# version 16
# baseline (speedup 1.0000x reference)
import numpy as np
import ml_dtypes
import sys

sys.path.insert(0, "/opt/trn_rl_repo")

import concourse.bass as bass
import concourse.mybir as mybir
from concourse.bass_utils import run_bass_kernel_spmd
from concourse.aot_env import allow_ici_all_to_all

F32 = mybir.dt.float32
BF16 = mybir.dt.bfloat16
F16 = mybir.dt.float16
I32 = mybir.dt.int32
U16 = mybir.dt.uint16
U8 = mybir.dt.uint8
AF = mybir.ActivationFunctionType
ALU = mybir.AluOpType
AX = mybir.AxisListType
BFNP = ml_dtypes.bfloat16

RS2 = 0.08838834764831845
CSHIFT = -3.0


class Geom:
    def __init__(self, S=4096, I=5632, D=2048):
        self.S, self.I, self.D = S, I, D
        self.H, self.DH = 16, 128
        self.CW = S // 8
        self.TSL = 2 * self.CW
        self.NTT = self.TSL // 128
        self.NG = S // 512
        self.NTH = self.TSL // 512
        assert self.TSL % 512 == 0
        self.NDC = D // 128
        self.NJ = I // 128
        self.GU_PC = 2 * I // 8
        self.NOB = self.GU_PC // 128
        self.DN_PC = D // 8
        self.NDB = self.DN_PC // 128


FULL = Geom()
CONST_VALS = [1.0, -1.0, 16384.0, 1e-5, 1e-6, RS2, CSHIFT, 0.0]
CIDX = {v: i for i, v in enumerate(CONST_VALS)}
K = 1024


class Plan:
    ENGINES = ("sync", "tensor", "vector", "scalar", "gpsimd")

    def __init__(self, nc, es):
        self.nc = nc
        self.ops = {e: [] for e in self.ENGINES}
        self.sem_handles = {}
        self.sem_count = {}
        self.es = es

    def sem(self, name):
        if name not in self.sem_handles:
            self.sem_handles[name] = self.es.enter_context(self.nc.semaphore(name))
            self.sem_count[name] = 0
        return self.sem_handles[name]

    def emit(self, eng, fn, waits=(), inc=None, inc_by=1):
        flat = []
        for w in waits:
            if w is None:
                continue
            if isinstance(w, list):
                flat += [x for x in w if x is not None]
            else:
                flat.append(w)
        tok = None
        if inc is not None:
            inc = f"{inc}@{eng}"
            self.sem(inc)
            self.sem_count[inc] += inc_by
            tok = (inc, self.sem_count[inc])
        for (s, v) in flat:
            assert self.sem_count.get(s, -1) >= v, f"wait on future {s}>={v}"
        self.ops[eng].append((fn, tuple(flat), inc, inc_by))
        return tok

    def dma(self, eng, out, in_, waits=(), inc="DMA", **kw):
        return self.emit(eng, lambda e: e.dma_start(out=out, in_=in_, **kw),
                         waits=waits, inc=inc, inc_by=16)

    def bar(self, name):
        toks = []
        for s, v in self.sem_count.items():
            if s.split("@")[0] == name and v > 0:
                toks.append((s, v))
        return toks

    def wait_only(self, eng, waits):
        flat = []
        for w in waits:
            if w is None:
                continue
            if isinstance(w, list):
                flat += [x for x in w if x is not None]
            else:
                flat.append(w)
        self.ops[eng].append((None, tuple(flat), None, 0))

    def replay(self, block):
        plan = self

        def make_body(name):
            def body(eng):
                waited = {}
                for fn, waits, inc, inc_by in plan.ops[name]:
                    for (s, v) in waits:
                        if waited.get(s, 0) < v:
                            eng.wait_ge(plan.sem_handles[s], v)
                            waited[s] = v
                    if fn is None:
                        continue
                    ins = fn(eng)
                    if inc is not None:
                        ins.then_inc(plan.sem_handles[inc], inc_by)
            return body

        block.sync(make_body("sync"))
        block.tensor(make_body("tensor"))
        block.vector(make_body("vector"))
        block.scalar(make_body("scalar"))
        block.gpsimd(make_body("gpsimd"))


class Guard:

    def __init__(self, n=1):
        self.last = [[] for _ in range(n)]

    def war(self, i=0):
        return list(self.last[i % len(self.last)])

    def used(self, tok, i=0):
        if tok is None:
            return
        self.last[i % len(self.last)] = list(tok) if isinstance(tok, list) else [tok]

    def also(self, tok, i=0):
        if tok is None:
            return
        if isinstance(tok, list):
            self.last[i % len(self.last)] += tok
        else:
            self.last[i % len(self.last)].append(tok)


def build_program(g: Geom, upto=9, dbg=(), no_cc=False):
    import contextlib
    es = contextlib.ExitStack()
    nc = bass.Bass("TRN2", target_bir_lowering=False, debug=False)

    S, I, D = g.S, g.I, g.D
    DH, TSL, NTT, NG, NTH = g.DH, g.TSL, g.NTT, g.NG, g.NTH
    NDC, NJ, NOB, NDB = g.NDC, g.NJ, g.NOB, g.NDB

    ap = lambda t: t.ap()
    x_b = ap(nc.dram_tensor("x_b", [S, D], F32, kind="ExternalInput"))
    x_sl = ap(nc.dram_tensor("x_sl", [TSL, D], F32, kind="ExternalInput"))
    wq_in = ap(nc.dram_tensor("wq", [4 * DH, D], F32, kind="ExternalInput"))
    wk_in = ap(nc.dram_tensor("wk", [4 * DH, D], F32, kind="ExternalInput"))
    wv_in = ap(nc.dram_tensor("wv", [4 * DH, D], F32, kind="ExternalInput"))
    wo_in = ap(nc.dram_tensor("wo", [D, D], F32, kind="ExternalInput"))
    gup = ap(nc.dram_tensor("gup", [g.GU_PC, D // 8], I32, kind="ExternalInput"))
    gus = ap(nc.dram_tensor("gus", [D // 128, g.GU_PC], F16, kind="ExternalInput"))
    dnp = ap(nc.dram_tensor("dnp", [g.DN_PC, I // 8], I32, kind="ExternalInput"))
    dns = ap(nc.dram_tensor("dns", [I // 128, g.DN_PC], F16, kind="ExternalInput"))
    consts_in = ap(nc.dram_tensor("consts", [128, 16], F32, kind="ExternalInput"))
    ident_in = ap(nc.dram_tensor("ident", [128, 128], BF16, kind="ExternalInput"))
    identf_in = ap(nc.dram_tensor("identf", [128, 128], F32, kind="ExternalInput"))
    tri_in = ap(nc.dram_tensor("tri", [128, 896], BF16, kind="ExternalInput"))
    onesrow_in = ap(nc.dram_tensor("onesrow", [1, 128], F32, kind="ExternalInput"))

    out_slab = ap(nc.dram_tensor("out_slab", [TSL, D], F32, kind="ExternalOutput"))

    wt_scr = ap(nc.dram_tensor("wt_scr", [12 * 128, D], BF16))
    xfm_spill = ap(nc.dram_tensor("xfm_spill", [NDC * 128, S], BF16))
    x1_spill = ap(nc.dram_tensor("x1_spill", [TSL, D], F32))
    cc_gu_in = nc.dram_tensor("cc_gu_in", [NOB * 128, D], BF16)
    cc_gu_out = nc.dram_tensor("cc_gu_out", [8 * NOB * 128, D], BF16)
    cc_dn_in = nc.dram_tensor("cc_dn_in", [NDB * 128, I], BF16)
    cc_dn_out = nc.dram_tensor("cc_dn_out", [8 * NDB * 128, I], BF16)
    cc_z_in = nc.dram_tensor("cc_z_in", [8 * 512, g.CW], BF16)
    cc_z_out = nc.dram_tensor("cc_z_out", [8 * 512, g.CW], BF16)

    dbg = dict(dbg)
    dbg_outs = {name: ap(nc.dram_tensor("dbg_" + name, list(shape), dt,
                                        kind="ExternalOutput"))
                for name, (shape, dt) in dbg.items()}

    sbt = lambda nm, sh, dt: es.enter_context(nc.sbuf_tensor(nm, sh, dt))
    sconst = sbt("sconst", [128, 16], F32)
    for v, i in CIDX.items():
        nc.const_aps.register(F32, float(v), sconst[:, i:i + 1])
    sident = sbt("sident", [128, 128], BF16)
    sidentf = sbt("sidentf", [128, 128], F32)
    stri = sbt("stri", [128, 896], BF16)
    sonesrow = sbt("sonesrow", [1, 128], F32)
    sstat = sbt("sstat", [128, 48], F32)
    sr_inv = sbt("sr_inv", [1, 512], F32)
    sjunk = sbt("sjunk", [128, 2048], F32)
    sscal = sbt("sscal", [128, NOB * NDC + NDB * NJ], F32)
    sc_l16 = sbt("sc_l16", [max(NDC, NJ), g.GU_PC], F16)
    sc_f = sbt("sc_f", [max(NDC, NJ), g.GU_PC], F32)

    arena = sbt("arena", [128, 150 * K], U8)

    def carve(off, nbytes, dt):
        size = {F32: 4, BF16: 2, F16: 2, I32: 4, U16: 2, U8: 1}[dt]
        assert off % 64 == 0 and off + nbytes <= 150 * K, (off, nbytes)
        v = arena.bitcast(dt)
        return v[:, off // size: (off + nbytes) // size]

    pb = [es.enter_context(nc.psum_tensor(f"pb{i}", [128, 512], F32)) for i in range(8)]
    pbg = [Guard(1) for _ in range(8)]

    P = Plan(nc, es)
    ones_col = sconst[:, CIDX[1.0]:CIDX[1.0] + 1]
    sonesb = sbt("sonesb", [128, 1], BF16)
    sonesrow_b = sbt("sonesrow_b", [1, 128], BF16)
    sr_inv_b = [sbt("sr_inv_b0", [1, 512], BF16), sbt("sr_inv_b1", [1, 512], BF16)]
    sr_inv2 = [sbt("sr_inv20", [1, 512], F32), sbt("sr_inv21", [1, 512], F32)]
    zst2 = [sbt("zst20", [128, 512], BF16), sbt("zst21", [128, 512], BF16)]
    zn2 = [sbt("zn20", [128, 512], BF16), sbt("zn21", [128, 512], BF16)]

    for dst, src in [(sconst[:, :], consts_in), (sident[:, :], ident_in),
                     (sidentf[:, :], identf_in), (stri[:, :], tri_in),
                     (sonesrow[:, :], onesrow_in)]:
        P.dma("sync", dst, src, inc="LDC")
    tLD0 = P.bar("LDC")
    tONB = P.emit("vector", lambda e: e.tensor_copy(sonesb[:, :], ones_col),
                  waits=[tLD0], inc="ONB")
    tONB2 = P.emit("vector", lambda e: e.tensor_copy(sonesrow_b[:, :], sonesrow[:, :]),
                   waits=[tLD0], inc="ONB")
    tLD0 = tLD0 + [tONB, tONB2]

    sq_war = {}
    st_war = {}

    def ln_stats(x_ap, width, col, eps_val, waits, site):
        c = col
        s1, s2 = sstat[:, c + 2:c + 3], sstat[:, c + 3:c + 4]
        mean, msq = sstat[:, c + 4:c + 5], sstat[:, c + 5:c + 6]
        var, std = sstat[:, c + 6:c + 7], sstat[:, c + 7:c + 8]
        war = st_war.get(site, [])
        tv = P.emit("vector", lambda e: e.reduce_sum(s1, x_ap, axis=AX.X),
                    waits=list(waits) + war, inc="LNS")
        ts = P.emit("scalar", lambda e: e.activation(
            sjunk[:, 0:width], x_ap, AF.Square, accum_out=s2),
            waits=list(waits) + war, inc="LNS")
        tv = P.emit("vector", lambda e: e.tensor_scalar(
            mean, s1, 1.0 / width, None, ALU.mult), waits=[tv], inc="LNS")
        tv = P.emit("vector", lambda e: e.tensor_tensor(msq, mean, mean, ALU.mult),
                    waits=[tv], inc="LNS")
        tv = P.emit("vector", lambda e: e.tensor_scalar(
            var, s2, 1.0 / width, msq, ALU.mult, ALU.subtract),
            waits=[tv, ts], inc="LNS")
        ts = P.emit("scalar", lambda e: e.activation(std, var, AF.Sqrt, bias=eps_val),
                    waits=[tv], inc="LNS")
        tv = P.emit("vector", lambda e: e.reciprocal(sstat[:, c:c + 1], std),
                    waits=[ts], inc="LNS")
        tv = P.emit("vector", lambda e: e.tensor_scalar(
            sstat[:, c + 1:c + 2], mean, sstat[:, c:c + 1], -1.0, ALU.mult, ALU.mult),
            waits=[tv], inc="LNS")
        return tv

    def ln_mark_done(site, tok):
        st_war[site] = [tok]

    def transpose16(src_fn, n, bank_i, dst_fn, waits, extra_first=()):
        bk = pb[bank_i]
        tev = None
        for i0 in range(0, n, 4):
            cnt = min(4, n - i0)
            tmm = None
            for kk in range(cnt):
                w = list(waits) + (pbg[bank_i].war() if tev is None else [tev])
                if tev is None:
                    for x_ in extra_first:
                        if isinstance(x_, list):
                            w += x_
                        elif x_ is not None:
                            w.append(x_)
                tmm = P.emit("tensor", lambda e, i=i0 + kk, kk=kk: e.matmul(
                    bk[:, kk * 128:(kk + 1) * 128], src_fn(i), sident[:, :],
                    start=True, stop=True), waits=w, inc="TRMM")
            tev = P.emit("vector", lambda e, i0=i0, cnt=cnt: e.tensor_copy(
                dst_fn(i0, cnt), bk[:, 0:cnt * 128]), waits=[tmm], inc="TREV")
        pbg[bank_i].used(tev)
        return tev

    wload = carve(52 * K, 8192, F32)
    wbf = carve(60 * K, 4096, BF16)
    wtst = carve(64 * K, 4096, BF16)

    t_spill, t_tc = None, None
    for pi, wsrc in enumerate((wq_in, wk_in, wv_in)):
        for h in range(4):
            r = pi * 4 + h
            tl = P.dma("sync", wload, wsrc[h * DH:(h + 1) * DH, :], inc="LDW",
                       waits=[t_tc])
            t_tc = P.emit("vector", lambda e: e.tensor_copy(wbf, wload),
                          waits=[tl, t_spill], inc="WCP")
            tev = transpose16(lambda i: wbf[:, i * 128:(i + 1) * 128], NDC, 7,
                              lambda i0, cnt: wtst[:, i0 * 128:(i0 + cnt) * 128],
                              waits=[t_tc, tLD0])
            t_spill = P.dma("sync", wt_scr[r * 128:(r + 1) * 128, :],
                            wtst[:, 0:NDC * 128], waits=[tev], inc="WPREP")
    tWT = P.bar("WPREP")

    tsl_ = P.dma("sync", sc_l16[0:D // 128, 0:g.GU_PC], gus, inc="SCL")
    tsc = P.emit("vector", lambda e: e.tensor_copy(
        sc_f[0:D // 128, 0:g.GU_PC], sc_l16[0:D // 128, 0:g.GU_PC]),
        waits=[tsl_], inc="SCV")
    t_ev = None
    for ob in range(NOB):
        tmm = P.emit("tensor", lambda e, ob=ob: e.matmul(
            pb[5][:, 0:NDC], sc_f[0:NDC, ob * 128:(ob + 1) * 128],
            sidentf[0:NDC, 0:NDC], start=True, stop=True),
            waits=[tsc, tLD0, t_ev] + pbg[5].war(), inc="SCMM")
        t_ev = P.emit("vector", lambda e, ob=ob: e.tensor_scalar(
            sscal[:, ob * NDC:(ob + 1) * NDC], pb[5][:, 0:NDC], 16384.0, None,
            ALU.mult), waits=[tmm], inc="SCEV")
        pbg[5].used(t_ev)
    tsl2 = P.dma("sync", sc_l16[0:NJ, 0:g.DN_PC], dns, waits=[tsc], inc="SCL")
    tsc2 = P.emit("vector", lambda e: e.tensor_copy(
        sc_f[0:NJ, 0:g.DN_PC], sc_l16[0:NJ, 0:g.DN_PC]), waits=[tsl2], inc="SCV")
    DNC0 = NOB * NDC
    for db in range(NDB):
        tmm = P.emit("tensor", lambda e, db=db: e.matmul(
            pb[5][:, 0:NJ], sc_f[0:NJ, db * 128:(db + 1) * 128],
            sidentf[0:NJ, 0:NJ], start=True, stop=True),
            waits=[tsc2, t_ev] + pbg[5].war(), inc="SCMM")
        t_ev = P.emit("vector", lambda e, db=db: e.tensor_scalar(
            sscal[:, DNC0 + db * NJ:DNC0 + (db + 1) * NJ], pb[5][:, 0:NJ],
            16384.0, None, ALU.mult), waits=[tmm], inc="SCEV")
        pbg[5].used(t_ev)
    tSCAL = t_ev

    dqw = carve(134 * K, 512, I32)
    dqc = carve(134 * K + 512, 512, I32)
    dqd = carve(135 * K, 512, I32)
    dqst = carve(135 * K + 512, 4096, I32)
    dqu = carve(139 * K + 512, 2048, U16)
    dqbf = [carve(141 * K + 512, 2048, BF16), carve(143 * K + 512, 2048, BF16)]
    dqtr = [carve(145 * K + 512, 2048, BF16), carve(147 * K + 512, 2048, BF16)]
    DQ = "vector"
    dqst3 = dqst.rearrange("p (w k) -> p w k", k=8)

    dq_bf_war = [[], []]
    dq_tr_war = [[], []]

    def dq_stage_a(idx, words_src, scal_base, nw, waits):
        pi = idx % 2
        ng = 8 * nw // 128
        tl = P.dma("sync", dqw[:, 0:nw], words_src, waits=waits, inc="DQL")
        t = tl
        for kk in range(8):
            t = P.emit(DQ, lambda e, kk=kk: e.tensor_scalar(
                dqc[:, 0:nw], dqw[:, 0:nw], 4 * kk, 15,
                ALU.logical_shift_right, ALU.bitwise_and), waits=[t], inc="DQI")
            t = P.emit(DQ, lambda e: e.tensor_scalar(
                dqd[:, 0:nw], dqc[:, 0:nw], 8, None, ALU.bitwise_and),
                waits=[t], inc="DQI")
            t = P.emit(DQ, lambda e: e.scalar_tensor_tensor(
                dqd[:, 0:nw], dqd[:, 0:nw], 7, dqc[:, 0:nw], ALU.mult, ALU.add),
                waits=[t], inc="DQI")
            t = P.emit(DQ, lambda e, kk=kk: e.tensor_scalar(
                dqst3[:, 0:nw, kk:kk + 1], dqd[:, 0:nw], 9, None,
                ALU.logical_shift_left), waits=[t], inc="DQI")
        t = P.emit(DQ, lambda e: e.tensor_copy(dqu[:, 0:8 * nw], dqst[:, 0:8 * nw]),
                   waits=[t], inc="DQI")
        tb = t
        for gg in range(ng):
            tb = P.emit("vector", lambda e, gg=gg, pi=pi: e.tensor_scalar(
                dqbf[pi][:, gg * 128:(gg + 1) * 128],
                dqu[:, gg * 128:(gg + 1) * 128].bitcast(F16),
                sscal[:, scal_base + gg:scal_base + gg + 1], None, ALU.mult),
                waits=([t, tSCAL] + dq_bf_war[pi]) if gg == 0 else [tb], inc="DQS")
        return tb

    def dq_stage_b(idx, cc_dst, nw, ta):
        pi = idx % 2
        ng = 8 * nw // 128
        tev = transpose16(lambda i, pi=pi: dqbf[pi][:, i * 128:(i + 1) * 128], ng, 5,
                          lambda i0, cnt, pi=pi: dqtr[pi][:, i0 * 128:(i0 + cnt) * 128],
                          waits=[ta], extra_first=dq_tr_war[pi])
        dq_bf_war[pi] = [tev]
        tdma = P.dma("sync", cc_dst, dqtr[pi][:, 0:8 * nw], waits=[tev], inc="CCW")
        dq_tr_war[pi] = [tdma]
        return tdma

    dq_specs = []
    for ob in range(NOB):
        for ch in range(D // 1024):
            dq_specs.append((
                gup[ob * 128:(ob + 1) * 128, ch * 128:(ch + 1) * 128],
                ob * NDC + ch * 8,
                cc_gu_in.ap()[ob * 128:(ob + 1) * 128, ch * 1024:(ch + 1) * 1024],
                128))
    for db in range(NDB):
        nwords_total = I // 8
        ch = 0
        w0 = 0
        while w0 < nwords_total:
            nw = min(128, nwords_total - w0)
            dq_specs.append((
                dnp[db * 128:(db + 1) * 128, w0:w0 + nw],
                DNC0 + db * NJ + w0 // 16,
                cc_dn_in.ap()[db * 128:(db + 1) * 128, w0 * 8:w0 * 8 + nw * 8],
                nw))
            w0 += nw
    dq_state = {"a": 0, "b": 0, "ta": {}, "t": None}

    def emit_dq(k):
        for _ in range(k):
            if dq_state["a"] < len(dq_specs) and dq_state["a"] - dq_state["b"] < 2:
                i = dq_state["a"]
                src, base, dst, nw = dq_specs[i]
                dq_state["ta"][i] = dq_stage_a(
                    i, src, base, nw,
                    waits=[dq_state.get("ap"), dq_state["t"]])
                dq_state["ap"] = dq_state["ta"][i]
                dq_state["a"] += 1
            if dq_state["b"] < dq_state["a"] - 1 or (
                    dq_state["a"] == len(dq_specs) and dq_state["b"] < dq_state["a"]):
                i = dq_state["b"]
                _, _, dst, nw = dq_specs[i]
                dq_state["t"] = dq_stage_b(i, dst, nw, dq_state["ta"].pop(i))
                dq_state["b"] += 1

    def dq_flush():
        while dq_state["b"] < len(dq_specs):
            emit_dq(2)

    xload = [carve(0, 8192, F32), carve(8 * K, 8192, F32),
             carve(68 * K, 8192, F32), carve(76 * K, 8192, F32)]
    flnb = [carve(16 * K, 4096, BF16), carve(84 * K, 4096, BF16)]
    dfm = [carve(20 * K, 16384, BF16), carve(36 * K, 16384, BF16)]
    dfm3 = [b.rearrange("p (hc t) -> p hc t", t=512) for b in dfm]
    xsp3 = xfm_spill.rearrange("(hc p) s -> p hc s", p=128)
    xlg = Guard(4)
    flng = Guard(2)
    t_xsp = {}

    for tt in range(S // 128):
        tg, qq = tt // 4, tt % 4
        bi, xi = tg % 2, tt % 4
        fi = tt % 2
        fln = flnb[fi]
        tl = P.dma("sync", xload[xi], x_b[tt * 128:(tt + 1) * 128, :], inc=f"XLD{xi}",
                   waits=xlg.war(xi))
        tst = ln_stats(xload[xi], D, 8 * (tt % 4), 1e-5, waits=[tl, tLD0],
                       site=("ln1", tt % 4))
        tnm = P.emit("vector", lambda e, xi=xi, fln=fln, c=8 * (tt % 4): e.tensor_scalar(
            fln, xload[xi], sstat[:, c:c + 1], sstat[:, c + 1:c + 2],
            ALU.mult, ALU.add), waits=[tst] + flng.war(fi), inc="LN1N")
        xlg.used(tnm, xi)
        ln_mark_done(("ln1", tt % 4), tnm)
        tev = transpose16(
            lambda i, fln=fln: fln[:, i * 128:(i + 1) * 128], NDC, 6,
            lambda i0, cnt, bi=bi, qq=qq: dfm3[bi][:, i0:i0 + cnt,
                                                   qq * 128:(qq + 1) * 128],
            waits=[tnm, tLD0], extra_first=[t_xsp.get(tg - 2)])
        flng.used(tev, fi)
        if qq == 3:
            t_xsp[tg] = P.dma("scalar", xsp3[:, :, tg * 512:(tg + 1) * 512],
                              dfm3[bi][:, :, :], waits=[tev], inc=f"XSP{tg % 2}")
    tXSP = P.bar("XSP0") + P.bar("XSP1")
    if "xfm" in dbg_outs:
        P.dma("sync", dbg_outs["xfm"], xfm_spill, waits=[tXSP], inc="DBG")

    def emit_cc1():
        dq_flush()
        tCCW = P.bar("CCW")

        def cc1_fn(e):
            return e.collective_compute(
                "AllGather", ALU.bypass, replica_groups=[list(range(8))],
                ins=[cc_gu_in.ap().opt()], outs=[cc_gu_out.ap().opt()])

        def cc1b_fn(e):
            return e.collective_compute(
                "AllGather", ALU.bypass, replica_groups=[list(range(8))],
                ins=[cc_dn_in.ap().opt()], outs=[cc_dn_out.ap().opt()])
        if no_cc:
            t = P.emit("gpsimd", lambda e: e.engine_nop(), waits=[tCCW], inc="CC1")
        else:
            P.emit("gpsimd", cc1_fn, waits=[tCCW], inc="CC1A")
            t = P.emit("gpsimd", cc1b_fn, waits=[tCCW], inc="CC1")
        if "gu_w" in dbg_outs:
            P.dma("sync", dbg_outs["gu_w"], cc_gu_out.ap(), waits=[t], inc="DBG")
        if "dn_w" in dbg_outs:
            P.dma("sync", dbg_outs["dn_w"], cc_dn_out.ap(), waits=[t], inc="DBG")
        return t

    if upto <= 1:
        emit_cc1()
        return _finish(P, nc, es, dbg_outs, None)

    v_sb = carve(0, 32 * K, BF16)
    q_sb = carve(32 * K, 16 * K, BF16)
    k_sb = carve(48 * K, 16 * K, BF16)
    xq = [carve(64 * K, 16 * K, BF16), carve(80 * K, 16 * K, BF16)]
    wqT = carve(96 * K, 8 * K, BF16)
    wkT = carve(104 * K, 8 * K, BF16)
    wvT = carve(112 * K, 16 * K, BF16)
    wvT3 = wvT.rearrange("p (hc x) -> p hc x", x=512)
    ebuf = [carve(128 * K + i * 1024, 1024, BF16) for i in range(6)]
    v4 = v_sb.rearrange("p (h x) -> p h x", h=4)

    ebg = Guard(6)
    zstg = Guard(2)
    zng = Guard(2)
    srg = Guard(2)
    xqg = Guard(2)
    wTg = Guard(1)
    t_zdma = None
    t_att_done = None

    for pair in range(2):
        twl = []
        for hl in range(2):
            h = pair * 2 + hl
            twl.append(P.dma("sync", wqT[:, hl * D:(hl + 1) * D],
                             wt_scr[h * 128:(h + 1) * 128, :],
                             waits=[tWT, t_att_done] + wTg.war(), inc="WTL"))
            twl.append(P.dma("sync", wkT[:, hl * D:(hl + 1) * D],
                             wt_scr[(4 + h) * 128:(4 + h + 1) * 128, :],
                             waits=[tWT, t_att_done] + wTg.war(), inc="WTL"))
        if pair == 0:
            for h in range(4):
                src = wt_scr[(8 + h) * 128:(8 + h + 1) * 128, :].rearrange(
                    "p (hc dh) -> p hc dh", dh=128)
                twl.append(P.dma("sync", wvT3[:, :, h * 128:(h + 1) * 128], src,
                                 waits=[tWT, tXSP], inc="WTL"))
        tWTL = P.bar("WTL")

        t_qev, t_kev, t_vev = {}, {}, {}
        for tg in range(NG):
            xi = tg % 2
            txq = P.dma("sync", xq[xi][:, :], xsp3[:, :, tg * 512:(tg + 1) * 512],
                        waits=[tXSP, tWT] + xqg.war(xi), inc=f"XQL{xi}")
            xq3 = xq[xi].rearrange("p (hc t) -> p hc t", t=512)
            for hl in range(2):
                tq = tk = None
                for hc in range(NDC):
                    tq = P.emit("tensor", lambda e, hl=hl, hc=hc, xq3=xq3: e.matmul(
                        pb[4][:, :], wqT[:, hl * D + hc * 128:hl * D + (hc + 1) * 128],
                        xq3[:, hc, :], start=(hc == 0), stop=(hc == NDC - 1)),
                        waits=[txq, tWTL] + pbg[4].war(), inc="QKM")
                tqe = P.emit("vector", lambda e, hl=hl, tg=tg: e.tensor_copy(
                    q_sb[:, hl * S + tg * 512:hl * S + (tg + 1) * 512], pb[4][:, :]),
                    waits=[tq], inc="QKE")
                pbg[4].used(tqe)
                t_qev[(hl, tg)] = tqe
                for hc in range(NDC):
                    tk = P.emit("tensor", lambda e, hl=hl, hc=hc, xq3=xq3: e.matmul(
                        pb[5][:, :], wkT[:, hl * D + hc * 128:hl * D + (hc + 1) * 128],
                        xq3[:, hc, :], start=(hc == 0), stop=(hc == NDC - 1)),
                        waits=[txq, tWTL] + pbg[5].war(), inc="QKM")
                tke = P.emit("vector", lambda e, hl=hl, tg=tg: e.tensor_copy(
                    k_sb[:, hl * S + tg * 512:hl * S + (tg + 1) * 512], pb[5][:, :]),
                    waits=[tk], inc="QKE")
                pbg[5].used(tke)
                t_kev[(hl, tg)] = tke
            if pair == 0:
                for q4 in range(4):
                    tt_ = tg * 4 + q4
                    tv = None
                    for hc in range(NDC):
                        tv = P.emit("tensor", lambda e, hc=hc, q4=q4, xq3=xq3: e.matmul(
                            pb[6][:, :], xq3[:, hc, q4 * 128:(q4 + 1) * 128],
                            wvT3[:, hc, :], start=(hc == 0), stop=(hc == NDC - 1)),
                            waits=[txq, tWTL] + pbg[6].war(), inc="QKM")
                    tve = None
                    for h4 in range(4):
                        tve = P.emit("vector", lambda e, tt_=tt_, h4=h4: e.tensor_copy(
                            v_sb[:, h4 * S + tt_ * 128:h4 * S + (tt_ + 1) * 128],
                            pb[6][:, h4 * 128:(h4 + 1) * 128]),
                            waits=[tv], inc="QKE")
                    pbg[6].used(tve)
                    t_vev[tt_] = tve
            xqg.used(t_kev[(1, tg)], xi)
            if pair == 0:
                xqg.also(t_vev[tg * 4 + 3], xi)
            emit_dq(1)
        tQKV = P.bar("QKE")

        zb_h = [4, 6]
        rs_sl = [pb[5][0:1, :], pb[5][64:65, :]]
        for gq in range(NG):
            njt = 4 * gq + 4
            t_exp = [{}, {}]
            t_av = [{}, {}]
            t_rsm = [{}, {}]
            for jt in range(njt + 1):
                if jt < njt:
                    for hl in range(2):
                        bi = 2 * hl + (jt % 2)
                        ei = 3 * hl + (jt % 3)
                        tsm = P.emit("tensor", lambda e, jt=jt, bi=bi, hl=hl, gq=gq: e.matmul(
                            pb[bi][:, :],
                            k_sb[:, hl * S + jt * 128:hl * S + (jt + 1) * 128],
                            q_sb[:, hl * S + gq * 512:hl * S + (gq + 1) * 512],
                            start=True, stop=True),
                            waits=[tQKV] + pbg[bi].war(), inc="SMM")
                        te = P.emit("scalar", lambda e, bi=bi, ei=ei: e.activation(
                            ebuf[ei][:, :], pb[bi][:, :], AF.Exp, bias=CSHIFT, scale=RS2),
                            waits=[tsm] + ebg.war(ei), inc="EXP")
                        pbg[bi].used(te)
                        if jt >= 4 * gq:
                            o = 128 * jt - 512 * gq
                            te = P.emit("vector", lambda e, ei=ei, o=o: e.tensor_tensor(
                                ebuf[ei][:, :], ebuf[ei][:, :],
                                stri[:, 384 - o:384 - o + 512], ALU.mult),
                                waits=[te], inc="MSK")
                        t_exp[hl][jt] = te
                if jt >= 1:
                    ja = jt - 1
                    for hl in range(2):
                        h = pair * 2 + hl
                        ei = 3 * hl + (ja % 3)
                        t_av[hl][ja] = P.emit(
                            "tensor", lambda e, ja=ja, ei=ei, h=h, njt=njt, hl=hl: e.matmul(
                                pb[zb_h[hl]][:, :],
                                v_sb[:, h * S + ja * 128:h * S + (ja + 1) * 128],
                                ebuf[ei][:, :], start=(ja == 0), stop=(ja == njt - 1)),
                            waits=[t_exp[hl][ja]] + (pbg[zb_h[hl]].war() if ja == 0 else []),
                            inc="AVM")
                        t_rsm[hl][ja] = P.emit(
                            "tensor", lambda e, ja=ja, ei=ei, njt=njt, hl=hl: e.matmul(
                                rs_sl[hl], sonesb[:, :], ebuf[ei][:, :],
                                start=(ja == 0), stop=(ja == njt - 1)),
                            waits=[t_exp[hl][ja]] + (pbg[5].war() if ja == 0 and hl == 0 else []),
                            inc="RSM")
                        ebg.used(t_rsm[hl][ja], ei)
                        ebg.also(t_av[hl][ja], ei)
            t_ln5 = None
            for hl in range(2):
                h = pair * 2 + hl
                t_ln = P.emit("scalar", lambda e, hl=hl: e.activation(
                    sr_inv2[hl][:, :], rs_sl[hl], AF.Ln),
                    waits=[t_rsm[hl][njt - 1]] + srg.war(hl), inc="RIV")
                t_ln5 = t_ln
                t_riv = P.emit("scalar", lambda e, hl=hl: e.activation(
                    sr_inv_b[hl][:, :], sr_inv2[hl][:, :], AF.Exp, scale=-1.0),
                    waits=[t_ln], inc="RIV")
                t_rbc = P.emit("tensor", lambda e, hl=hl: e.matmul(
                    pb[7][:, :], sonesrow_b[:, :], sr_inv_b[hl][:, :],
                    start=True, stop=True),
                    waits=[t_riv] + pbg[7].war(), inc="RBC")
                srg.used(t_rbc, hl)
                t_zbf = P.emit("scalar", lambda e, hl=hl: e.copy(
                    zst2[hl][:, :], pb[zb_h[hl]][:, :]),
                    waits=[t_av[hl][njt - 1]] + zstg.war(hl), inc="ZBF")
                pbg[zb_h[hl]].used(t_zbf)
                t_znm = P.emit("vector", lambda e, hl=hl: e.scalar_tensor_tensor(
                    zn2[hl][:, :], zst2[hl][:, :], 1.0, pb[7][:, :],
                    ALU.mult, ALU.mult),
                    waits=[t_zbf, t_rbc] + zng.war(hl), inc="ZNM")
                pbg[7].used(t_znm)
                zstg.used(t_znm, hl)
                hlh = pair * 2 + hl
                CW = g.CW
                nsub = 512 // CW if CW < 512 else 1
                for sb_ in range(nsub):
                    i_rank = (gq * 512 + sb_ * CW) // CW if CW < 512 else gq
                    wdt = min(CW, 512)
                    t_zdma = P.dma(
                        "sync",
                        cc_z_in.ap()[i_rank * 512 + hlh * 128:
                                     i_rank * 512 + (hlh + 1) * 128, 0:wdt],
                        zn2[hl][:, sb_ * wdt:(sb_ + 1) * wdt],
                        waits=[t_znm] + zng.war(hl), inc=f"ZDM{hlh}")
                zng.used(t_zdma, hl)
            pbg[5].used(t_ln5)
            emit_dq(4)
        t_att_done = sum((P.bar(f"ZDM{i}") for i in range(4)), [])
        wTg.used(tQKV)
        if pair == 0:
            tCC1 = emit_cc1()
    tZDM = sum((P.bar(f"ZDM{i}") for i in range(4)), [])

    def cc2_fn(e):
        if no_cc:
            return e.engine_nop()
        with allow_ici_all_to_all():
            return e.collective_compute(
                "AllToAll", ALU.bypass, replica_groups=[list(range(8))],
                ins=[cc_z_in.ap().opt()], outs=[cc_z_out.ap().opt()])
    tCC2 = P.emit("gpsimd", cc2_fn, waits=[tZDM], inc="CC2")

    if "z" in dbg_outs:
        P.dma("sync", dbg_outs["z"], cc_z_in.ap(), waits=[tZDM], inc="DBG")
    if "q_sb" in dbg_outs:
        P.dma("sync", dbg_outs["q_sb"], q_sb[:, 0:2 * S], waits=[tZDM], inc="DBG")
    if "k_sb" in dbg_outs:
        P.dma("sync", dbg_outs["k_sb"], k_sb[:, 0:2 * S], waits=[tZDM], inc="DBG")
    if "v_sb" in dbg_outs:
        P.dma("sync", dbg_outs["v_sb"], v_sb[:, 0:4 * S], waits=[tZDM], inc="DBG")
    if "zg" in dbg_outs:
        P.dma("sync", dbg_outs["zg"], cc_z_out.ap(), waits=[tCC2], inc="DBG")
    if upto <= 2:
        return _finish(P, nc, es, dbg_outs, None)

    woT = carve(32 * K, 64 * K, BF16)
    wload2 = carve(128 * K, 8192, F32)
    wbf2 = carve(136 * K, 4096, BF16)
    t_woc, t_wol = None, None
    for dc in range(NDC):
        tl = P.dma("sync", wload2, wo_in[dc * 128:(dc + 1) * 128, :],
                   waits=[t_woc, tZDM, tCC1], inc="WOL")
        t_woc = P.emit("vector", lambda e: e.tensor_copy(wbf2, wload2),
                       waits=[tl, t_wol], inc="WOC")
        t_wol = transpose16(
            lambda i: wbf2[:, i * 128:(i + 1) * 128], NDC, 6,
            lambda i0, cnt, dc=dc: woT.rearrange(
                "p (fc x) -> p fc x", x=NDC * 128)[:, i0:i0 + cnt,
                                                   dc * 128:(dc + 1) * 128],
            waits=[t_woc, tZDM])
    tWOT = t_wol

    zg = carve(0, 32 * K, BF16)
    t_zgl = None
    for h in range(16):
        for bb in range(2):
            r = 4 * bb + h // 4
            hlh = h % 4
            t_zgl = P.dma("sync",
                          zg[:, h * TSL + bb * g.CW:h * TSL + (bb + 1) * g.CW],
                          cc_z_out.ap()[r * 512 + hlh * 128:r * 512 + (hlh + 1) * 128, :],
                          waits=[tCC2], inc="ZGL")
    tZGL = P.bar("ZGL")

    attn_fm = carve(96 * K, 32 * K, BF16)
    t_woev = {}
    for dc in range(NDC):
        for th in range(NTH):
            ii = dc * NTH + th
            bi = ii % 2
            tm = None
            for fc in range(NDC):
                tm = P.emit("tensor", lambda e, dc=dc, th=th, fc=fc, bi=bi: e.matmul(
                    pb[bi][:, :], woT[:, (fc * NDC + dc) * 128:(fc * NDC + dc + 1) * 128],
                    zg[:, fc * TSL + th * 512:fc * TSL + (th + 1) * 512],
                    start=(fc == 0), stop=(fc == NDC - 1)),
                    waits=[tZGL, tWOT] + pbg[bi].war(), inc="WOM")
            tev = P.emit("scalar", lambda e, dc=dc, th=th, bi=bi: e.copy(
                attn_fm[:, dc * TSL + th * 512:dc * TSL + (th + 1) * 512], pb[bi][:, :]),
                waits=[tm], inc="WOE")
            pbg[bi].used(tev)
            t_woev[ii] = tev
    tWOE = P.bar("WOE")
    if "attn_fm" in dbg_outs:
        P.dma("sync", dbg_outs["attn_fm"], attn_fm[:, 0:NDC * TSL],
              waits=[tWOE], inc="DBG")

    xfm2 = carve(0, 32 * K, BF16)
    xfm23 = xfm2.rearrange("p (hc t) -> p hc t", t=TSL)
    atile = [carve(32 * K, 4096, BF16), carve(36 * K, 4096, BF16)]
    xl2t = [carve(40 * K, 4096, BF16), carve(44 * K, 4096, BF16)]
    x1t = [carve(48 * K, 8192, F32), carve(56 * K, 8192, F32)]
    xtile = [carve(64 * K, 8192, F32), carve(72 * K, 8192, F32)]
    lntmp = carve(80 * K, 8192, F32)
    atg = Guard(2)
    x1g = Guard(2)
    xtg = Guard(2)
    lng = Guard(1)
    xf2g = Guard(1)

    t_x1s = {}
    for tt in range(NTT):
        ai = tt % 2
        tev = transpose16(
            lambda i, tt=tt: attn_fm[:, i * TSL + tt * 128:i * TSL + tt * 128 + 128],
            NDC, 6,
            lambda i0, cnt, ai=ai: atile[ai][:, i0 * 128:(i0 + cnt) * 128],
            waits=[tWOE], extra_first=atg.war(ai))
        tst = ln_stats(atile[ai][:, 0:D], D, 8 * (tt % 2), 1e-5, waits=[tev],
                       site=("lnp", tt % 2))
        tl = P.dma("sync", xtile[ai], x_sl[tt * 128:(tt + 1) * 128, :],
                   waits=[tWOE] + xtg.war(ai), inc=f"X2L{ai}")
        tnm = P.emit("vector", lambda e, ai=ai, c=8 * (tt % 2): e.tensor_scalar(
            lntmp[:, 0:D], atile[ai][:, 0:D], sstat[:, c:c + 1],
            sstat[:, c + 1:c + 2], ALU.mult, ALU.add),
            waits=[tst] + lng.war(), inc="LNP")
        atg.used(tnm, ai)
        ln_mark_done(("lnp", tt % 2), tnm)
        tx1 = P.emit("vector", lambda e, ai=ai: e.tensor_tensor(
            x1t[ai][:, 0:D], lntmp[:, 0:D], xtile[ai], ALU.add),
            waits=[tnm, tl] + x1g.war(ai), inc="X1")
        lng.used(tx1)
        xtg.used(tx1, ai)
        tst2 = ln_stats(x1t[ai][:, 0:D], D, 16 + 8 * (tt % 2), 1e-5, waits=[tx1],
                        site=("ln2", tt % 2))
        tn2 = P.emit("vector", lambda e, ai=ai, c=16 + 8 * (tt % 2): e.tensor_scalar(
            xl2t[ai][:, 0:D], x1t[ai][:, 0:D], sstat[:, c:c + 1],
            sstat[:, c + 1:c + 2], ALU.mult, ALU.add),
            waits=[tst2], inc="LN2N")
        ln_mark_done(("ln2", tt % 2), tn2)
        t_x1s[tt] = P.dma("sync", x1_spill[tt * 128:(tt + 1) * 128, :],
                          x1t[ai][:, 0:D], waits=[tn2], inc=f"X1S{ai}")
        x1g.used(t_x1s[tt], ai)
        txf = transpose16(
            lambda i, ai=ai: xl2t[ai][:, i * 128:(i + 1) * 128], NDC, 7,
            lambda i0, cnt, tt=tt: xfm23[:, i0:i0 + cnt, tt * 128:(tt + 1) * 128],
            waits=[tn2, tZGL, tWOE], extra_first=xf2g.war())
        xf2g.used(txf)
    tXF2 = P.bar("TREV")
    tX1S = P.bar("X1S0") + P.bar("X1S1")
    if "x1" in dbg_outs:
        P.dma("sync", dbg_outs["x1"], x1_spill, waits=[tX1S], inc="DBG")
    if "xfm2" in dbg_outs:
        P.dma("sync", dbg_outs["xfm2"], xfm2[:, 0:NDC * TSL], waits=[tXF2], inc="DBG")
    if upto <= 3:
        return _finish(P, nc, es, dbg_outs, None)

    h_fm = carve(32 * K, NJ * TSL * 2, BF16)
    gwb = [carve(120 * K, 4096, BF16), carve(124 * K, 4096, BF16)]
    uwb = [carve(128 * K, 4096, BF16), carve(132 * K, 4096, BF16)]
    stmp = [carve(136 * K, 2048, F32), carve(138 * K, 2048, F32)]
    gwg = Guard(2)
    uwg = Guard(2)
    stg = Guard(2)

    t_h = {}
    for j in range(NJ):
        bi = j % 2
        tg1 = P.dma("sync", gwb[bi][:, 0:D], cc_gu_out.ap()[j * 128:(j + 1) * 128, :],
                    waits=[tCC1, tX1S, tXF2] + gwg.war(bi), inc=f"GWLg{bi}")
        tg2 = P.dma("sync", uwb[bi][:, 0:D],
                    cc_gu_out.ap()[(NJ + j) * 128:(NJ + j + 1) * 128, :],
                    waits=[tCC1, tX1S, tXF2] + uwg.war(bi), inc=f"GWLu{bi}")
        for th in range(NTH):
            tgm = tum = None
            for hc in range(NDC):
                tgm = P.emit("tensor", lambda e, j=j, th=th, hc=hc, bi=bi: e.matmul(
                    pb[bi][:, :], gwb[bi][:, hc * 128:(hc + 1) * 128],
                    xfm23[:, hc, th * 512:(th + 1) * 512],
                    start=(hc == 0), stop=(hc == NDC - 1)),
                    waits=[tg1, tXF2] + pbg[bi].war(), inc="GUM")
            for hc in range(NDC):
                tum = P.emit("tensor", lambda e, j=j, th=th, hc=hc, bi=bi: e.matmul(
                    pb[2 + bi][:, :], uwb[bi][:, hc * 128:(hc + 1) * 128],
                    xfm23[:, hc, th * 512:(th + 1) * 512],
                    start=(hc == 0), stop=(hc == NDC - 1)),
                    waits=[tg2, tXF2] + pbg[2 + bi].war(), inc="GUM")
            tsi = P.emit("scalar", lambda e, bi=bi, th=th: e.activation(
                stmp[th % 2][:, :], pb[bi][:, :], AF.Silu),
                waits=[tgm] + stg.war(th), inc="SIL")
            pbg[bi].used(tsi)
            th_ = P.emit("vector", lambda e, j=j, th=th, bi=bi: e.tensor_tensor(
                h_fm[:, j * TSL + th * 512:j * TSL + (th + 1) * 512],
                stmp[th % 2][:, :], pb[2 + bi][:, :], ALU.mult),
                waits=[tsi, tum], inc="HDN")
            pbg[2 + bi].used(th_)
            stg.used(th_, th)
            t_h[(j, th)] = th_
        gwg.used(t_h[(j, NTH - 1)], bi)
        uwg.used(t_h[(j, NTH - 1)], bi)
    tH = P.bar("HDN")
    if "h" in dbg_outs:
        P.dma("sync", dbg_outs["h"], h_fm[:, 0:NJ * TSL], waits=[tH], inc="DBG")

    wdn = [carve(120 * K, I * 2, BF16), carve(120 * K + ((I * 2 + 63) // 64) * 64, I * 2, BF16)]
    y_fm = carve(0, 32 * K, BF16)
    dwg = Guard(2)
    t_ye = {}
    for dc in range(NDC):
        bi = dc % 2
        tdw = P.dma("sync", wdn[bi][:, 0:I], cc_dn_out.ap()[dc * 128:(dc + 1) * 128, :],
                    waits=[tCC1, tH] + dwg.war(bi), inc=f"DWL{bi}")
        tevp = None
        for th in range(NTH):
            pbi = 4 + (dc * NTH + th) % 2
            tm = None
            for j in range(NJ):
                tm = P.emit("tensor", lambda e, dc=dc, th=th, j=j, bi=bi, pbi=pbi: e.matmul(
                    pb[pbi][:, :], wdn[bi][:, j * 128:(j + 1) * 128],
                    h_fm[:, j * TSL + th * 512:j * TSL + (th + 1) * 512],
                    start=(j == 0), stop=(j == NJ - 1)),
                    waits=[tdw, tH] + pbg[pbi].war(), inc="DYM")
            tevp = P.emit("scalar", lambda e, dc=dc, th=th, pbi=pbi: e.copy(
                y_fm[:, dc * TSL + th * 512:dc * TSL + (th + 1) * 512], pb[pbi][:, :]),
                waits=[tm], inc="YEV")
            pbg[pbi].used(tevp)
        t_ye[dc] = tevp
        dwg.used(tevp, bi)
    tY = P.bar("YEV")
    if "y_fm" in dbg_outs:
        P.dma("sync", dbg_outs["y_fm"], y_fm[:, 0:NDC * TSL], waits=[tY], inc="DBG")

    lntmp2 = carve(32 * K, 8192, F32)
    x1l = [carve(40 * K, 8192, F32), carve(48 * K, 8192, F32)]
    ytile = [carve(56 * K, 4096, BF16), carve(60 * K, 4096, BF16)]
    ytg = Guard(2)
    x1lg = Guard(2)
    lng2 = Guard(1)
    t_out = {}
    for tt in range(NTT):
        ai = tt % 2
        tev = transpose16(
            lambda i, tt=tt: y_fm[:, i * TSL + tt * 128:i * TSL + tt * 128 + 128],
            NDC, 6, lambda i0, cnt, ai=ai: ytile[ai][:, i0 * 128:(i0 + cnt) * 128],
            waits=[tY], extra_first=ytg.war(ai))
        c = 8 * (tt % 2)
        war = st_war.get(("tail", tt % 2), [])
        s1, s2 = sstat[:, c + 2:c + 3], sstat[:, c + 3:c + 4]
        m_, ym2 = sstat[:, c + 4:c + 5], sstat[:, c + 5:c + 6]
        crms, rc = sstat[:, c + 6:c + 7], sstat[:, c + 7:c + 8]
        a2 = 32 + 4 * (tt % 2)
        var2, sig = sstat[:, a2:a2 + 1], sstat[:, a2 + 1:a2 + 2]
        alpha, beta = sstat[:, a2 + 2:a2 + 3], sstat[:, a2 + 3:a2 + 4]
        tv = P.emit("vector", lambda e, ai=ai: e.reduce_sum(
            s1, ytile[ai][:, 0:D], axis=AX.X), waits=[tev] + war, inc="LNS")
        ts = P.emit("scalar", lambda e, ai=ai: e.activation(
            sjunk[:, 0:D], ytile[ai][:, 0:D], AF.Square, accum_out=s2),
            waits=[tev] + war, inc="LNS")
        tv = P.emit("vector", lambda e: e.tensor_scalar(
            m_, s1, 1.0 / D, None, ALU.mult), waits=[tv], inc="LNS")
        tv = P.emit("vector", lambda e: e.tensor_scalar(
            ym2, s2, 1.0 / D, None, ALU.mult), waits=[tv, ts], inc="LNS")
        ts = P.emit("scalar", lambda e: e.activation(crms, ym2, AF.Sqrt, bias=1e-6),
                    waits=[tv], inc="LNS")
        tv = P.emit("vector", lambda e: e.reciprocal(rc, crms), waits=[ts], inc="LNS")
        tv = P.emit("vector", lambda e: e.tensor_tensor(var2, m_, m_, ALU.mult),
                    waits=[tv], inc="LNS")
        tv = P.emit("vector", lambda e: e.tensor_tensor(var2, ym2, var2, ALU.subtract),
                    waits=[tv], inc="LNS")
        tv = P.emit("vector", lambda e: e.tensor_tensor(sig, rc, rc, ALU.mult),
                    waits=[tv], inc="LNS")
        tv = P.emit("vector", lambda e: e.tensor_tensor(var2, var2, sig, ALU.mult),
                    waits=[tv], inc="LNS")
        ts = P.emit("scalar", lambda e: e.activation(sig, var2, AF.Sqrt, bias=1e-5),
                    waits=[tv], inc="LNS")
        tv = P.emit("vector", lambda e: e.reciprocal(var2, sig), waits=[ts], inc="LNS")
        tv = P.emit("vector", lambda e: e.tensor_tensor(alpha, rc, var2, ALU.mult),
                    waits=[tv], inc="LNS")
        tv = P.emit("vector", lambda e: e.tensor_scalar(
            beta, m_, var2, -1.0, ALU.mult, ALU.mult), waits=[tv], inc="LNS")
        tv = P.emit("vector", lambda e: e.tensor_tensor(beta, beta, rc, ALU.mult),
                    waits=[tv], inc="LNS")
        tl = P.dma("sync", x1l[ai], x1_spill[tt * 128:(tt + 1) * 128, :],
                   waits=[tX1S, tY] + x1lg.war(ai), inc=f"X1L{ai}")
        tapp = P.emit("vector", lambda e, ai=ai: e.tensor_scalar(
            lntmp2[:, 0:D], ytile[ai][:, 0:D], alpha, beta, ALU.mult, ALU.add),
            waits=[tv] + lng2.war(), inc="FAP")
        ytg.used(tapp, ai)
        st_war[("tail", tt % 2)] = [tapp]
        tfin = P.emit("vector", lambda e, ai=ai: e.tensor_tensor(
            lntmp2[:, 0:D], lntmp2[:, 0:D], x1l[ai], ALU.add),
            waits=[tapp, tl], inc="FIN")
        t_out[tt] = P.dma("sync", out_slab[tt * 128:(tt + 1) * 128, :], lntmp2[:, 0:D],
                          waits=[tfin], inc="OUT")
        lng2.used(t_out[tt])
        x1lg.used(t_out[tt], ai)
    return _finish(P, nc, es, dbg_outs, P.bar("OUT"))


def _finish(P, nc, es, dbg_outs, tOUT):
    if tOUT is not None:
        P.wait_only("sync", [tOUT])
    if dbg_outs:
        P.wait_only("sync", [P.bar("DBG")])
    block = es.enter_context(nc.Block())
    P.replay(block)
    es.close()
    return nc


def make_consts():
    consts = np.zeros((128, 16), dtype=np.float32)
    for v, i in CIDX.items():
        consts[:, i] = v
    ident = np.eye(128, dtype=np.float32).astype(BFNP)
    identf = np.eye(128, dtype=np.float32)
    p = np.arange(128)[:, None]
    m = np.arange(896)[None, :]
    tri = ((m - 384) >= p).astype(np.float32).astype(BFNP)
    onesrow = np.ones((1, 128), dtype=np.float32)
    return consts, ident, identf, tri, onesrow


def make_in_maps(g, x, W_Q, W_K, W_V, W_O, gup_full, gus_full, dnp_full, dns_full):
    consts, ident, identf, tri, onesrow = make_consts()
    expect = {
        "x_b": np.float32, "x_sl": np.float32, "wq": np.float32, "wk": np.float32,
        "wv": np.float32, "wo": np.float32, "gup": np.int32, "gus": np.float16,
        "dnp": np.int32, "dns": np.float16, "consts": np.float32,
        "ident": BFNP, "identf": np.float32, "tri": BFNP, "onesrow": np.float32,
    }
    in_maps = []
    for c in range(8):
        b, q = c // 4, c % 4
        m = {
            "x_b": np.ascontiguousarray(x[b]),
            "x_sl": np.ascontiguousarray(np.concatenate(
                [x[0][c * g.CW:(c + 1) * g.CW], x[1][c * g.CW:(c + 1) * g.CW]], 0)),
            "wq": np.ascontiguousarray(W_Q[4 * q:4 * q + 4].reshape(4 * g.DH, g.D)),
            "wk": np.ascontiguousarray(W_K[4 * q:4 * q + 4].reshape(4 * g.DH, g.D)),
            "wv": np.ascontiguousarray(W_V[4 * q:4 * q + 4].reshape(4 * g.DH, g.D)),
            "wo": np.ascontiguousarray(W_O),
            "gup": np.ascontiguousarray(gup_full[c * g.GU_PC:(c + 1) * g.GU_PC]),
            "gus": np.ascontiguousarray(gus_full[:, c * g.GU_PC:(c + 1) * g.GU_PC]),
            "dnp": np.ascontiguousarray(dnp_full[c * g.DN_PC:(c + 1) * g.DN_PC]),
            "dns": np.ascontiguousarray(dns_full[:, c * g.DN_PC:(c + 1) * g.DN_PC]),
            "consts": consts, "ident": ident, "identf": identf,
            "tri": tri, "onesrow": onesrow,
        }
        for k_, v_ in m.items():
            assert v_.dtype == expect[k_], (k_, v_.dtype)
        in_maps.append(m)
    return in_maps


def kernel(L0_attn_x, L0_attn_W_K, L0_attn_W_Q, L0_attn_W_V, L0_attn_W_O,
           L0_ffn_gate_up_packed, L0_ffn_gate_up_scales, L0_ffn_down_packed,
           L0_ffn_down_scales):
    g = FULL
    x = np.asarray(L0_attn_x)
    in_maps = make_in_maps(
        g, x, np.asarray(L0_attn_W_Q), np.asarray(L0_attn_W_K),
        np.asarray(L0_attn_W_V), np.asarray(L0_attn_W_O),
        np.asarray(L0_ffn_gate_up_packed), np.asarray(L0_ffn_gate_up_scales),
        np.asarray(L0_ffn_down_packed), np.asarray(L0_ffn_down_scales))
    nc = build_program(g)
    res = run_bass_kernel_spmd(nc, in_maps, core_ids=list(range(8)))
    out = np.empty((2, g.S, g.D), dtype=np.float32)
    for c in range(8):
        slab = res.results[c]["out_slab"]
        out[0, c * g.CW:(c + 1) * g.CW] = slab[0:g.CW]
        out[1, c * g.CW:(c + 1) * g.CW] = slab[g.CW:]
    return out


# version 17
# speedup vs baseline: 1.0476x; 1.0476x over previous
import numpy as np
import ml_dtypes
import sys

sys.path.insert(0, "/opt/trn_rl_repo")

import concourse.bass as bass
import concourse.mybir as mybir
from concourse.bass_utils import run_bass_kernel_spmd
from concourse.aot_env import allow_ici_all_to_all

F32 = mybir.dt.float32
BF16 = mybir.dt.bfloat16
F16 = mybir.dt.float16
I32 = mybir.dt.int32
U16 = mybir.dt.uint16
U8 = mybir.dt.uint8
AF = mybir.ActivationFunctionType
ALU = mybir.AluOpType
AX = mybir.AxisListType
BFNP = ml_dtypes.bfloat16

RS2 = 0.08838834764831845
CSHIFT = -3.0


class Geom:
    def __init__(self, S=4096, I=5632, D=2048):
        self.S, self.I, self.D = S, I, D
        self.H, self.DH = 16, 128
        self.CW = S // 8
        self.TSL = 2 * self.CW
        self.NTT = self.TSL // 128
        self.NG = S // 512
        self.NTH = self.TSL // 512
        assert self.TSL % 512 == 0
        self.NDC = D // 128
        self.NJ = I // 128
        self.GU_PC = 2 * I // 8
        self.NOB = self.GU_PC // 128
        self.DN_PC = D // 8
        self.NDB = self.DN_PC // 128


FULL = Geom()
CONST_VALS = [1.0, -1.0, 16384.0, 1e-5, 1e-6, RS2, CSHIFT, 0.0]
CIDX = {v: i for i, v in enumerate(CONST_VALS)}
K = 1024


class Plan:
    ENGINES = ("sync", "tensor", "vector", "scalar", "gpsimd")

    def __init__(self, nc, es):
        self.nc = nc
        self.ops = {e: [] for e in self.ENGINES}
        self.sem_handles = {}
        self.sem_count = {}
        self.es = es

    def sem(self, name):
        if name not in self.sem_handles:
            self.sem_handles[name] = self.es.enter_context(self.nc.semaphore(name))
            self.sem_count[name] = 0
        return self.sem_handles[name]

    def emit(self, eng, fn, waits=(), inc=None, inc_by=1):
        flat = []
        for w in waits:
            if w is None:
                continue
            if isinstance(w, list):
                flat += [x for x in w if x is not None]
            else:
                flat.append(w)
        tok = None
        if inc is not None:
            inc = f"{inc}@{eng}"
            self.sem(inc)
            self.sem_count[inc] += inc_by
            tok = (inc, self.sem_count[inc])
        for (s, v) in flat:
            assert self.sem_count.get(s, -1) >= v, f"wait on future {s}>={v}"
        self.ops[eng].append((fn, tuple(flat), inc, inc_by))
        return tok

    def dma(self, eng, out, in_, waits=(), inc="DMA", **kw):
        return self.emit(eng, lambda e: e.dma_start(out=out, in_=in_, **kw),
                         waits=waits, inc=inc, inc_by=16)

    def bar(self, name):
        toks = []
        for s, v in self.sem_count.items():
            if s.split("@")[0] == name and v > 0:
                toks.append((s, v))
        return toks

    def wait_only(self, eng, waits):
        flat = []
        for w in waits:
            if w is None:
                continue
            if isinstance(w, list):
                flat += [x for x in w if x is not None]
            else:
                flat.append(w)
        self.ops[eng].append((None, tuple(flat), None, 0))

    def replay(self, block):
        plan = self

        def make_body(name):
            def body(eng):
                waited = {}
                for fn, waits, inc, inc_by in plan.ops[name]:
                    for (s, v) in waits:
                        if waited.get(s, 0) < v:
                            eng.wait_ge(plan.sem_handles[s], v)
                            waited[s] = v
                    if fn is None:
                        continue
                    ins = fn(eng)
                    if inc is not None:
                        ins.then_inc(plan.sem_handles[inc], inc_by)
            return body

        block.sync(make_body("sync"))
        block.tensor(make_body("tensor"))
        block.vector(make_body("vector"))
        block.scalar(make_body("scalar"))
        block.gpsimd(make_body("gpsimd"))


class Guard:

    def __init__(self, n=1):
        self.last = [[] for _ in range(n)]

    def war(self, i=0):
        return list(self.last[i % len(self.last)])

    def used(self, tok, i=0):
        if tok is None:
            return
        self.last[i % len(self.last)] = list(tok) if isinstance(tok, list) else [tok]

    def also(self, tok, i=0):
        if tok is None:
            return
        if isinstance(tok, list):
            self.last[i % len(self.last)] += tok
        else:
            self.last[i % len(self.last)].append(tok)


def build_program(g: Geom, upto=9, dbg=(), no_cc=False):
    import contextlib
    es = contextlib.ExitStack()
    nc = bass.Bass("TRN2", target_bir_lowering=False, debug=False)

    S, I, D = g.S, g.I, g.D
    DH, TSL, NTT, NG, NTH = g.DH, g.TSL, g.NTT, g.NG, g.NTH
    NDC, NJ, NOB, NDB = g.NDC, g.NJ, g.NOB, g.NDB

    ap = lambda t: t.ap()
    x_b = ap(nc.dram_tensor("x_b", [S, D], F32, kind="ExternalInput"))
    x_sl = ap(nc.dram_tensor("x_sl", [TSL, D], F32, kind="ExternalInput"))
    wq_in = ap(nc.dram_tensor("wq", [4 * DH, D], F32, kind="ExternalInput"))
    wk_in = ap(nc.dram_tensor("wk", [4 * DH, D], F32, kind="ExternalInput"))
    wv_in = ap(nc.dram_tensor("wv", [4 * DH, D], F32, kind="ExternalInput"))
    wo_in = ap(nc.dram_tensor("wo", [D, D], F32, kind="ExternalInput"))
    gup = ap(nc.dram_tensor("gup", [g.GU_PC, D // 8], I32, kind="ExternalInput"))
    gus = ap(nc.dram_tensor("gus", [D // 128, g.GU_PC], F16, kind="ExternalInput"))
    dnp = ap(nc.dram_tensor("dnp", [g.DN_PC, I // 8], I32, kind="ExternalInput"))
    dns = ap(nc.dram_tensor("dns", [I // 128, g.DN_PC], F16, kind="ExternalInput"))
    consts_in = ap(nc.dram_tensor("consts", [128, 16], F32, kind="ExternalInput"))
    ident_in = ap(nc.dram_tensor("ident", [128, 128], BF16, kind="ExternalInput"))
    identf_in = ap(nc.dram_tensor("identf", [128, 128], F32, kind="ExternalInput"))
    tri_in = ap(nc.dram_tensor("tri", [128, 896], BF16, kind="ExternalInput"))
    onesrow_in = ap(nc.dram_tensor("onesrow", [1, 128], F32, kind="ExternalInput"))

    out_slab = ap(nc.dram_tensor("out_slab", [TSL, D], F32, kind="ExternalOutput"))

    wt_scr = ap(nc.dram_tensor("wt_scr", [12 * 128, D], BF16))
    xfm_spill = ap(nc.dram_tensor("xfm_spill", [NDC * 128, S], BF16))
    x1_spill = ap(nc.dram_tensor("x1_spill", [TSL, D], F32))
    cc_gu_in = nc.dram_tensor("cc_gu_in", [NOB * 128, D], BF16)
    cc_gu_out = nc.dram_tensor("cc_gu_out", [8 * NOB * 128, D], BF16)
    cc_dn_in = nc.dram_tensor("cc_dn_in", [NDB * 128, I], BF16)
    cc_dn_out = nc.dram_tensor("cc_dn_out", [8 * NDB * 128, I], BF16)
    cc_z_in = nc.dram_tensor("cc_z_in", [8 * 512, g.CW], BF16)
    cc_z_out = nc.dram_tensor("cc_z_out", [8 * 512, g.CW], BF16)

    dbg = dict(dbg)
    dbg_outs = {name: ap(nc.dram_tensor("dbg_" + name, list(shape), dt,
                                        kind="ExternalOutput"))
                for name, (shape, dt) in dbg.items()}

    sbt = lambda nm, sh, dt: es.enter_context(nc.sbuf_tensor(nm, sh, dt))
    sconst = sbt("sconst", [128, 16], F32)
    for v, i in CIDX.items():
        nc.const_aps.register(F32, float(v), sconst[:, i:i + 1])
    sident = sbt("sident", [128, 128], BF16)
    sidentf = sbt("sidentf", [128, 128], F32)
    stri = sbt("stri", [128, 896], BF16)
    sonesrow = sbt("sonesrow", [1, 128], F32)
    sstat = sbt("sstat", [128, 48], F32)
    sr_inv = sbt("sr_inv", [1, 512], F32)
    sjunk = sbt("sjunk", [128, 2048], F32)
    sscal = sbt("sscal", [128, NOB * NDC + NDB * NJ], F32)
    sc_l16 = sbt("sc_l16", [max(NDC, NJ), g.GU_PC], F16)
    sc_f = sbt("sc_f", [max(NDC, NJ), g.GU_PC], F32)

    arena = sbt("arena", [128, 150 * K], U8)

    def carve(off, nbytes, dt):
        size = {F32: 4, BF16: 2, F16: 2, I32: 4, U16: 2, U8: 1}[dt]
        assert off % 64 == 0 and off + nbytes <= 150 * K, (off, nbytes)
        v = arena.bitcast(dt)
        return v[:, off // size: (off + nbytes) // size]

    pb = [es.enter_context(nc.psum_tensor(f"pb{i}", [128, 512], F32)) for i in range(8)]
    pbg = [Guard(1) for _ in range(8)]

    P = Plan(nc, es)
    ones_col = sconst[:, CIDX[1.0]:CIDX[1.0] + 1]
    sonesb = sbt("sonesb", [128, 1], BF16)
    sonesrow_b = sbt("sonesrow_b", [1, 128], BF16)
    sr_inv_b = [sbt("sr_inv_b0", [1, 512], BF16), sbt("sr_inv_b1", [1, 512], BF16)]
    sr_inv2 = [sbt("sr_inv20", [1, 512], F32), sbt("sr_inv21", [1, 512], F32)]
    zst2 = [sbt("zst20", [128, 512], BF16), sbt("zst21", [128, 512], BF16)]
    zn2 = [sbt("zn20", [128, 512], BF16), sbt("zn21", [128, 512], BF16)]

    for dst, src in [(sconst[:, :], consts_in), (sident[:, :], ident_in),
                     (sidentf[:, :], identf_in), (stri[:, :], tri_in),
                     (sonesrow[:, :], onesrow_in)]:
        P.dma("sync", dst, src, inc="LDC")
    tLD0 = P.bar("LDC")
    tONB = P.emit("vector", lambda e: e.tensor_copy(sonesb[:, :], ones_col),
                  waits=[tLD0], inc="ONB")
    tONB2 = P.emit("vector", lambda e: e.tensor_copy(sonesrow_b[:, :], sonesrow[:, :]),
                   waits=[tLD0], inc="ONB")
    tLD0 = tLD0 + [tONB, tONB2]

    sq_war = {}
    st_war = {}

    def ln_stats(x_ap, width, col, eps_val, waits, site):
        c = col
        s1, s2 = sstat[:, c + 2:c + 3], sstat[:, c + 3:c + 4]
        mean, msq = sstat[:, c + 4:c + 5], sstat[:, c + 5:c + 6]
        var, std = sstat[:, c + 6:c + 7], sstat[:, c + 7:c + 8]
        war = st_war.get(site, [])
        tv = P.emit("vector", lambda e: e.reduce_sum(s1, x_ap, axis=AX.X),
                    waits=list(waits) + war, inc="LNS")
        ts = P.emit("scalar", lambda e: e.activation(
            sjunk[:, 0:width], x_ap, AF.Square, accum_out=s2),
            waits=list(waits) + war, inc="LNS")
        tv = P.emit("vector", lambda e: e.tensor_scalar(
            mean, s1, 1.0 / width, None, ALU.mult), waits=[tv], inc="LNS")
        tv = P.emit("vector", lambda e: e.tensor_tensor(msq, mean, mean, ALU.mult),
                    waits=[tv], inc="LNS")
        tv = P.emit("vector", lambda e: e.tensor_scalar(
            var, s2, 1.0 / width, msq, ALU.mult, ALU.subtract),
            waits=[tv, ts], inc="LNS")
        ts = P.emit("scalar", lambda e: e.activation(std, var, AF.Sqrt, bias=eps_val),
                    waits=[tv], inc="LNS")
        tv = P.emit("vector", lambda e: e.reciprocal(sstat[:, c:c + 1], std),
                    waits=[ts], inc="LNS")
        tv = P.emit("vector", lambda e: e.tensor_scalar(
            sstat[:, c + 1:c + 2], mean, sstat[:, c:c + 1], -1.0, ALU.mult, ALU.mult),
            waits=[tv], inc="LNS")
        return tv

    def ln_mark_done(site, tok):
        st_war[site] = [tok]

    def transpose16(src_fn, n, bank_i, dst_fn, waits, extra_first=()):
        bk = pb[bank_i]
        tev = None
        for i0 in range(0, n, 4):
            cnt = min(4, n - i0)
            tmm = None
            for kk in range(cnt):
                w = list(waits) + (pbg[bank_i].war() if tev is None else [tev])
                if tev is None:
                    for x_ in extra_first:
                        if isinstance(x_, list):
                            w += x_
                        elif x_ is not None:
                            w.append(x_)
                tmm = P.emit("tensor", lambda e, i=i0 + kk, kk=kk: e.matmul(
                    bk[:, kk * 128:(kk + 1) * 128], src_fn(i), sident[:, :],
                    start=True, stop=True), waits=w, inc="TRMM")
            tev = P.emit("vector", lambda e, i0=i0, cnt=cnt: e.tensor_copy(
                dst_fn(i0, cnt), bk[:, 0:cnt * 128]), waits=[tmm], inc="TREV")
        pbg[bank_i].used(tev)
        return tev

    wload = carve(52 * K, 8192, F32)
    wbf = carve(60 * K, 4096, BF16)
    wtst = carve(64 * K, 4096, BF16)

    t_spill, t_tc = None, None
    for pi, wsrc in enumerate((wq_in, wk_in, wv_in)):
        for h in range(4):
            r = pi * 4 + h
            tl = P.dma("sync", wload, wsrc[h * DH:(h + 1) * DH, :], inc="LDW",
                       waits=[t_tc])
            t_tc = P.emit("vector", lambda e: e.tensor_copy(wbf, wload),
                          waits=[tl, t_spill], inc="WCP")
            tev = transpose16(lambda i: wbf[:, i * 128:(i + 1) * 128], NDC, 7,
                              lambda i0, cnt: wtst[:, i0 * 128:(i0 + cnt) * 128],
                              waits=[t_tc, tLD0])
            t_spill = P.dma("sync", wt_scr[r * 128:(r + 1) * 128, :],
                            wtst[:, 0:NDC * 128], waits=[tev], inc="WPREP")
    tWT = P.bar("WPREP")

    tsl_ = P.dma("sync", sc_l16[0:D // 128, 0:g.GU_PC], gus, inc="SCL")
    tsc = P.emit("vector", lambda e: e.tensor_copy(
        sc_f[0:D // 128, 0:g.GU_PC], sc_l16[0:D // 128, 0:g.GU_PC]),
        waits=[tsl_], inc="SCV")
    t_ev = None
    for ob in range(NOB):
        tmm = P.emit("tensor", lambda e, ob=ob: e.matmul(
            pb[5][:, 0:NDC], sc_f[0:NDC, ob * 128:(ob + 1) * 128],
            sidentf[0:NDC, 0:NDC], start=True, stop=True),
            waits=[tsc, tLD0, t_ev] + pbg[5].war(), inc="SCMM")
        t_ev = P.emit("vector", lambda e, ob=ob: e.tensor_scalar(
            sscal[:, ob * NDC:(ob + 1) * NDC], pb[5][:, 0:NDC], 16384.0, None,
            ALU.mult), waits=[tmm], inc="SCEV")
        pbg[5].used(t_ev)
    tsl2 = P.dma("sync", sc_l16[0:NJ, 0:g.DN_PC], dns, waits=[tsc], inc="SCL")
    tsc2 = P.emit("vector", lambda e: e.tensor_copy(
        sc_f[0:NJ, 0:g.DN_PC], sc_l16[0:NJ, 0:g.DN_PC]), waits=[tsl2], inc="SCV")
    DNC0 = NOB * NDC
    for db in range(NDB):
        tmm = P.emit("tensor", lambda e, db=db: e.matmul(
            pb[5][:, 0:NJ], sc_f[0:NJ, db * 128:(db + 1) * 128],
            sidentf[0:NJ, 0:NJ], start=True, stop=True),
            waits=[tsc2, t_ev] + pbg[5].war(), inc="SCMM")
        t_ev = P.emit("vector", lambda e, db=db: e.tensor_scalar(
            sscal[:, DNC0 + db * NJ:DNC0 + (db + 1) * NJ], pb[5][:, 0:NJ],
            16384.0, None, ALU.mult), waits=[tmm], inc="SCEV")
        pbg[5].used(t_ev)
    tSCAL = t_ev

    dqw = carve(134 * K, 512, I32)
    dqc = carve(134 * K + 512, 512, I32)
    dqd = carve(135 * K, 512, I32)
    dqst = carve(135 * K + 512, 4096, I32)
    dqu = carve(139 * K + 512, 2048, U16)
    dqbf = [carve(141 * K + 512, 2048, BF16), carve(143 * K + 512, 2048, BF16)]
    dqtr = [carve(145 * K + 512, 2048, BF16), carve(147 * K + 512, 2048, BF16)]
    DQ = "vector"
    dqst3 = dqst.rearrange("p (w k) -> p w k", k=8)

    dq_bf_war = [[], []]
    dq_tr_war = [[], []]

    def dq_stage_a(idx, words_src, scal_base, nw, waits):
        pi = idx % 2
        ng = 8 * nw // 128
        tl = P.dma("sync", dqw[:, 0:nw], words_src, waits=waits, inc="DQL")
        t = tl
        for kk in range(8):
            t = P.emit(DQ, lambda e, kk=kk: e.tensor_scalar(
                dqc[:, 0:nw], dqw[:, 0:nw], 4 * kk, 15,
                ALU.logical_shift_right, ALU.bitwise_and), waits=[t], inc="DQI")
            t = P.emit(DQ, lambda e: e.tensor_scalar(
                dqd[:, 0:nw], dqc[:, 0:nw], 8, None, ALU.bitwise_and),
                waits=[t], inc="DQI")
            t = P.emit(DQ, lambda e: e.scalar_tensor_tensor(
                dqd[:, 0:nw], dqd[:, 0:nw], 7, dqc[:, 0:nw], ALU.mult, ALU.add),
                waits=[t], inc="DQI")
            t = P.emit(DQ, lambda e, kk=kk: e.tensor_scalar(
                dqst3[:, 0:nw, kk:kk + 1], dqd[:, 0:nw], 9, None,
                ALU.logical_shift_left), waits=[t], inc="DQI")
        t = P.emit(DQ, lambda e: e.tensor_copy(dqu[:, 0:8 * nw], dqst[:, 0:8 * nw]),
                   waits=[t], inc="DQI")
        tb = t
        for gg in range(ng):
            tb = P.emit("vector", lambda e, gg=gg, pi=pi: e.tensor_scalar(
                dqbf[pi][:, gg * 128:(gg + 1) * 128],
                dqu[:, gg * 128:(gg + 1) * 128].bitcast(F16),
                sscal[:, scal_base + gg:scal_base + gg + 1], None, ALU.mult),
                waits=([t, tSCAL] + dq_bf_war[pi]) if gg == 0 else [tb], inc="DQS")
        return tb

    def dq_stage_b(idx, cc_dst, nw, ta):
        pi = idx % 2
        ng = 8 * nw // 128
        tev = transpose16(lambda i, pi=pi: dqbf[pi][:, i * 128:(i + 1) * 128], ng, 5,
                          lambda i0, cnt, pi=pi: dqtr[pi][:, i0 * 128:(i0 + cnt) * 128],
                          waits=[ta], extra_first=dq_tr_war[pi])
        dq_bf_war[pi] = [tev]
        tdma = P.dma("sync", cc_dst, dqtr[pi][:, 0:8 * nw], waits=[tev], inc="CCW")
        dq_tr_war[pi] = [tdma]
        return tdma

    dq_specs = []
    for ob in range(NOB):
        for ch in range(D // 1024):
            dq_specs.append((
                gup[ob * 128:(ob + 1) * 128, ch * 128:(ch + 1) * 128],
                ob * NDC + ch * 8,
                cc_gu_in.ap()[ob * 128:(ob + 1) * 128, ch * 1024:(ch + 1) * 1024],
                128))
    for db in range(NDB):
        nwords_total = I // 8
        ch = 0
        w0 = 0
        while w0 < nwords_total:
            nw = min(128, nwords_total - w0)
            dq_specs.append((
                dnp[db * 128:(db + 1) * 128, w0:w0 + nw],
                DNC0 + db * NJ + w0 // 16,
                cc_dn_in.ap()[db * 128:(db + 1) * 128, w0 * 8:w0 * 8 + nw * 8],
                nw))
            w0 += nw
    dq_state = {"a": 0, "b": 0, "ta": {}, "t": None}

    def emit_dq(k):
        for _ in range(k):
            if dq_state["a"] < len(dq_specs) and dq_state["a"] - dq_state["b"] < 2:
                i = dq_state["a"]
                src, base, dst, nw = dq_specs[i]
                dq_state["ta"][i] = dq_stage_a(
                    i, src, base, nw,
                    waits=[dq_state.get("ap"), dq_state["t"]])
                dq_state["ap"] = dq_state["ta"][i]
                dq_state["a"] += 1
            if dq_state["b"] < dq_state["a"] - 1 or (
                    dq_state["a"] == len(dq_specs) and dq_state["b"] < dq_state["a"]):
                i = dq_state["b"]
                _, _, dst, nw = dq_specs[i]
                dq_state["t"] = dq_stage_b(i, dst, nw, dq_state["ta"].pop(i))
                dq_state["b"] += 1

    def dq_flush():
        while dq_state["b"] < len(dq_specs):
            emit_dq(2)

    xload = [carve(0, 8192, F32), carve(8 * K, 8192, F32),
             carve(68 * K, 8192, F32), carve(76 * K, 8192, F32)]
    flnb = [carve(16 * K, 4096, BF16), carve(84 * K, 4096, BF16)]
    dfm = [carve(20 * K, 16384, BF16), carve(36 * K, 16384, BF16)]
    dfm3 = [b.rearrange("p (hc t) -> p hc t", t=512) for b in dfm]
    xsp3 = xfm_spill.rearrange("(hc p) s -> p hc s", p=128)
    xlg = Guard(4)
    flng = Guard(2)
    t_xsp = {}

    for tt in range(S // 128):
        tg, qq = tt // 4, tt % 4
        bi, xi = tg % 2, tt % 4
        fi = tt % 2
        fln = flnb[fi]
        tl = P.dma("sync", xload[xi], x_b[tt * 128:(tt + 1) * 128, :], inc=f"XLD{xi}",
                   waits=xlg.war(xi))
        tst = ln_stats(xload[xi], D, 8 * (tt % 4), 1e-5, waits=[tl, tLD0],
                       site=("ln1", tt % 4))
        tnm = P.emit("vector", lambda e, xi=xi, fln=fln, c=8 * (tt % 4): e.tensor_scalar(
            fln, xload[xi], sstat[:, c:c + 1], sstat[:, c + 1:c + 2],
            ALU.mult, ALU.add), waits=[tst] + flng.war(fi), inc="LN1N")
        xlg.used(tnm, xi)
        ln_mark_done(("ln1", tt % 4), tnm)
        tev = transpose16(
            lambda i, fln=fln: fln[:, i * 128:(i + 1) * 128], NDC, 6,
            lambda i0, cnt, bi=bi, qq=qq: dfm3[bi][:, i0:i0 + cnt,
                                                   qq * 128:(qq + 1) * 128],
            waits=[tnm, tLD0], extra_first=[t_xsp.get(tg - 2)])
        flng.used(tev, fi)
        if qq == 3:
            t_xsp[tg] = P.dma("scalar", xsp3[:, :, tg * 512:(tg + 1) * 512],
                              dfm3[bi][:, :, :], waits=[tev], inc=f"XSP{tg % 2}")
    tXSP = P.bar("XSP0") + P.bar("XSP1")
    if "xfm" in dbg_outs:
        P.dma("sync", dbg_outs["xfm"], xfm_spill, waits=[tXSP], inc="DBG")

    def emit_cc1():
        dq_flush()
        tCCW = P.bar("CCW")

        def cc1_fn(e):
            return e.collective_compute(
                "AllGather", ALU.bypass, replica_groups=[list(range(8))],
                ins=[cc_gu_in.ap().opt()], outs=[cc_gu_out.ap().opt()])

        def cc1b_fn(e):
            return e.collective_compute(
                "AllGather", ALU.bypass, replica_groups=[list(range(8))],
                ins=[cc_dn_in.ap().opt()], outs=[cc_dn_out.ap().opt()])
        if no_cc:
            t = P.emit("gpsimd", lambda e: e.engine_nop(), waits=[tCCW], inc="CC1")
        else:
            P.emit("gpsimd", cc1_fn, waits=[tCCW], inc="CC1A")
            t = P.emit("gpsimd", cc1b_fn, waits=[tCCW], inc="CC1")
        if "gu_w" in dbg_outs:
            P.dma("sync", dbg_outs["gu_w"], cc_gu_out.ap(), waits=[t], inc="DBG")
        if "dn_w" in dbg_outs:
            P.dma("sync", dbg_outs["dn_w"], cc_dn_out.ap(), waits=[t], inc="DBG")
        return t

    if upto <= 1:
        emit_cc1()
        return _finish(P, nc, es, dbg_outs, None)

    v_sb = carve(0, 32 * K, BF16)
    q_sb = carve(32 * K, 16 * K, BF16)
    k_sb = carve(48 * K, 16 * K, BF16)
    xq = [carve(64 * K, 16 * K, BF16), carve(80 * K, 16 * K, BF16)]
    wqT = carve(96 * K, 8 * K, BF16)
    wkT = carve(104 * K, 8 * K, BF16)
    wvT = carve(112 * K, 16 * K, BF16)
    wvT3 = wvT.rearrange("p (hc x) -> p hc x", x=512)
    ebuf = [carve(128 * K + i * 1024, 1024, BF16) for i in range(6)]
    v4 = v_sb.rearrange("p (h x) -> p h x", h=4)

    ebg = Guard(6)
    zstg = Guard(2)
    zng = Guard(2)
    srg = Guard(2)
    xqg = Guard(2)
    wTg = Guard(1)
    t_zdma = None
    t_att_done = None

    for pair in range(2):
        twl = []
        for hl in range(2):
            h = pair * 2 + hl
            twl.append(P.dma("sync", wqT[:, hl * D:(hl + 1) * D],
                             wt_scr[h * 128:(h + 1) * 128, :],
                             waits=[tWT, t_att_done] + wTg.war(), inc="WTL"))
            twl.append(P.dma("sync", wkT[:, hl * D:(hl + 1) * D],
                             wt_scr[(4 + h) * 128:(4 + h + 1) * 128, :],
                             waits=[tWT, t_att_done] + wTg.war(), inc="WTL"))
        if pair == 0:
            for h in range(4):
                src = wt_scr[(8 + h) * 128:(8 + h + 1) * 128, :].rearrange(
                    "p (hc dh) -> p hc dh", dh=128)
                twl.append(P.dma("sync", wvT3[:, :, h * 128:(h + 1) * 128], src,
                                 waits=[tWT, tXSP], inc="WTL"))
        tWTL = P.bar("WTL")

        t_qev, t_kev, t_vev = {}, {}, {}
        for tg in range(NG):
            xi = tg % 2
            txq = P.dma("sync", xq[xi][:, :], xsp3[:, :, tg * 512:(tg + 1) * 512],
                        waits=[tXSP, tWT] + xqg.war(xi), inc=f"XQL{xi}")
            xq3 = xq[xi].rearrange("p (hc t) -> p hc t", t=512)
            for hl in range(2):
                tq = tk = None
                for hc in range(NDC):
                    tq = P.emit("tensor", lambda e, hl=hl, hc=hc, xq3=xq3: e.matmul(
                        pb[4][:, :], wqT[:, hl * D + hc * 128:hl * D + (hc + 1) * 128],
                        xq3[:, hc, :], start=(hc == 0), stop=(hc == NDC - 1)),
                        waits=[txq, tWTL] + pbg[4].war(), inc="QKM")
                tqe = P.emit("scalar", lambda e, hl=hl, tg=tg: e.copy(
                    q_sb[:, hl * S + tg * 512:hl * S + (tg + 1) * 512], pb[4][:, :]),
                    waits=[tq], inc="QKE")
                pbg[4].used(tqe)
                t_qev[(hl, tg)] = tqe
                for hc in range(NDC):
                    tk = P.emit("tensor", lambda e, hl=hl, hc=hc, xq3=xq3: e.matmul(
                        pb[5][:, :], wkT[:, hl * D + hc * 128:hl * D + (hc + 1) * 128],
                        xq3[:, hc, :], start=(hc == 0), stop=(hc == NDC - 1)),
                        waits=[txq, tWTL] + pbg[5].war(), inc="QKM")
                tke = P.emit("scalar", lambda e, hl=hl, tg=tg: e.copy(
                    k_sb[:, hl * S + tg * 512:hl * S + (tg + 1) * 512], pb[5][:, :]),
                    waits=[tk], inc="QKE")
                pbg[5].used(tke)
                t_kev[(hl, tg)] = tke
            if pair == 0:
                for q4 in range(4):
                    tt_ = tg * 4 + q4
                    tv = None
                    for hc in range(NDC):
                        tv = P.emit("tensor", lambda e, hc=hc, q4=q4, xq3=xq3: e.matmul(
                            pb[6][:, :], xq3[:, hc, q4 * 128:(q4 + 1) * 128],
                            wvT3[:, hc, :], start=(hc == 0), stop=(hc == NDC - 1)),
                            waits=[txq, tWTL] + pbg[6].war(), inc="QKM")
                    tve = None
                    for h4 in range(4):
                        tve = P.emit("scalar", lambda e, tt_=tt_, h4=h4: e.copy(
                            v_sb[:, h4 * S + tt_ * 128:h4 * S + (tt_ + 1) * 128],
                            pb[6][:, h4 * 128:(h4 + 1) * 128]),
                            waits=[tv], inc="QKE")
                    pbg[6].used(tve)
                    t_vev[tt_] = tve
            xqg.used(t_kev[(1, tg)], xi)
            if pair == 0:
                xqg.also(t_vev[tg * 4 + 3], xi)
            emit_dq(1)
        tQKV = P.bar("QKE")

        zb_h = [4, 6]
        rs_sl = [pb[5][0:1, :], pb[5][64:65, :]]
        for gq in range(NG):
            njt = 4 * gq + 4
            t_exp = [{}, {}]
            t_av = [{}, {}]
            t_rsm = [{}, {}]
            for jt in range(njt + 1):
                if jt < njt:
                    for hl in range(2):
                        bi = 2 * hl + (jt % 2)
                        ei = 3 * hl + (jt % 3)
                        tsm = P.emit("tensor", lambda e, jt=jt, bi=bi, hl=hl, gq=gq: e.matmul(
                            pb[bi][:, :],
                            k_sb[:, hl * S + jt * 128:hl * S + (jt + 1) * 128],
                            q_sb[:, hl * S + gq * 512:hl * S + (gq + 1) * 512],
                            start=True, stop=True),
                            waits=[tQKV] + pbg[bi].war(), inc="SMM")
                        te = P.emit("scalar", lambda e, bi=bi, ei=ei: e.activation(
                            ebuf[ei][:, :], pb[bi][:, :], AF.Exp, bias=CSHIFT, scale=RS2),
                            waits=[tsm] + ebg.war(ei), inc="EXP")
                        pbg[bi].used(te)
                        if jt >= 4 * gq:
                            o = 128 * jt - 512 * gq
                            te = P.emit("vector", lambda e, ei=ei, o=o: e.tensor_tensor(
                                ebuf[ei][:, :], ebuf[ei][:, :],
                                stri[:, 384 - o:384 - o + 512], ALU.mult),
                                waits=[te], inc="MSK")
                        t_exp[hl][jt] = te
                if jt >= 1:
                    ja = jt - 1
                    for hl in range(2):
                        h = pair * 2 + hl
                        ei = 3 * hl + (ja % 3)
                        t_av[hl][ja] = P.emit(
                            "tensor", lambda e, ja=ja, ei=ei, h=h, njt=njt, hl=hl: e.matmul(
                                pb[zb_h[hl]][:, :],
                                v_sb[:, h * S + ja * 128:h * S + (ja + 1) * 128],
                                ebuf[ei][:, :], start=(ja == 0), stop=(ja == njt - 1)),
                            waits=[t_exp[hl][ja]] + (pbg[zb_h[hl]].war() if ja == 0 else []),
                            inc="AVM")
                        t_rsm[hl][ja] = P.emit(
                            "tensor", lambda e, ja=ja, ei=ei, njt=njt, hl=hl: e.matmul(
                                rs_sl[hl], sonesb[:, :], ebuf[ei][:, :],
                                start=(ja == 0), stop=(ja == njt - 1)),
                            waits=[t_exp[hl][ja]] + (pbg[5].war() if ja == 0 and hl == 0 else []),
                            inc="RSM")
                        ebg.used(t_rsm[hl][ja], ei)
                        ebg.also(t_av[hl][ja], ei)
            t_ln5 = None
            for hl in range(2):
                h = pair * 2 + hl
                t_ln = P.emit("scalar", lambda e, hl=hl: e.activation(
                    sr_inv2[hl][:, :], rs_sl[hl], AF.Ln),
                    waits=[t_rsm[hl][njt - 1]] + srg.war(hl), inc="RIV")
                t_ln5 = t_ln
                t_riv = P.emit("scalar", lambda e, hl=hl: e.activation(
                    sr_inv_b[hl][:, :], sr_inv2[hl][:, :], AF.Exp, scale=-1.0),
                    waits=[t_ln], inc="RIV")
                t_rbc = P.emit("tensor", lambda e, hl=hl: e.matmul(
                    pb[7][:, :], sonesrow_b[:, :], sr_inv_b[hl][:, :],
                    start=True, stop=True),
                    waits=[t_riv] + pbg[7].war(), inc="RBC")
                srg.used(t_rbc, hl)
                t_zbf = P.emit("scalar", lambda e, hl=hl: e.copy(
                    zst2[hl][:, :], pb[zb_h[hl]][:, :]),
                    waits=[t_av[hl][njt - 1]] + zstg.war(hl), inc="ZBF")
                pbg[zb_h[hl]].used(t_zbf)
                t_znm = P.emit("vector", lambda e, hl=hl: e.scalar_tensor_tensor(
                    zn2[hl][:, :], zst2[hl][:, :], 1.0, pb[7][:, :],
                    ALU.mult, ALU.mult),
                    waits=[t_zbf, t_rbc] + zng.war(hl), inc="ZNM")
                pbg[7].used(t_znm)
                zstg.used(t_znm, hl)
                hlh = pair * 2 + hl
                CW = g.CW
                nsub = 512 // CW if CW < 512 else 1
                for sb_ in range(nsub):
                    i_rank = (gq * 512 + sb_ * CW) // CW if CW < 512 else gq
                    wdt = min(CW, 512)
                    t_zdma = P.dma(
                        "sync",
                        cc_z_in.ap()[i_rank * 512 + hlh * 128:
                                     i_rank * 512 + (hlh + 1) * 128, 0:wdt],
                        zn2[hl][:, sb_ * wdt:(sb_ + 1) * wdt],
                        waits=[t_znm] + zng.war(hl), inc=f"ZDM{hlh}")
                zng.used(t_zdma, hl)
            pbg[5].used(t_ln5)
            emit_dq(4)
        t_att_done = sum((P.bar(f"ZDM{i}") for i in range(4)), [])
        wTg.used(tQKV)
        if pair == 0:
            tCC1 = emit_cc1()
    tZDM = sum((P.bar(f"ZDM{i}") for i in range(4)), [])

    def cc2_fn(e):
        if no_cc:
            return e.engine_nop()
        with allow_ici_all_to_all():
            return e.collective_compute(
                "AllToAll", ALU.bypass, replica_groups=[list(range(8))],
                ins=[cc_z_in.ap().opt()], outs=[cc_z_out.ap().opt()])
    tCC2 = P.emit("gpsimd", cc2_fn, waits=[tZDM], inc="CC2")

    if "z" in dbg_outs:
        P.dma("sync", dbg_outs["z"], cc_z_in.ap(), waits=[tZDM], inc="DBG")
    if "q_sb" in dbg_outs:
        P.dma("sync", dbg_outs["q_sb"], q_sb[:, 0:2 * S], waits=[tZDM], inc="DBG")
    if "k_sb" in dbg_outs:
        P.dma("sync", dbg_outs["k_sb"], k_sb[:, 0:2 * S], waits=[tZDM], inc="DBG")
    if "v_sb" in dbg_outs:
        P.dma("sync", dbg_outs["v_sb"], v_sb[:, 0:4 * S], waits=[tZDM], inc="DBG")
    if "zg" in dbg_outs:
        P.dma("sync", dbg_outs["zg"], cc_z_out.ap(), waits=[tCC2], inc="DBG")
    if upto <= 2:
        return _finish(P, nc, es, dbg_outs, None)

    woT = carve(32 * K, 64 * K, BF16)
    wload2 = carve(128 * K, 8192, F32)
    wbf2 = carve(136 * K, 4096, BF16)
    t_woc, t_wol = None, None
    for dc in range(NDC):
        tl = P.dma("sync", wload2, wo_in[dc * 128:(dc + 1) * 128, :],
                   waits=[t_woc, tZDM, tCC1], inc="WOL")
        t_woc = P.emit("vector", lambda e: e.tensor_copy(wbf2, wload2),
                       waits=[tl, t_wol], inc="WOC")
        t_wol = transpose16(
            lambda i: wbf2[:, i * 128:(i + 1) * 128], NDC, 6,
            lambda i0, cnt, dc=dc: woT.rearrange(
                "p (fc x) -> p fc x", x=NDC * 128)[:, i0:i0 + cnt,
                                                   dc * 128:(dc + 1) * 128],
            waits=[t_woc, tZDM])
    tWOT = t_wol

    zg = carve(0, 32 * K, BF16)
    t_zgl = None
    for h in range(16):
        for bb in range(2):
            r = 4 * bb + h // 4
            hlh = h % 4
            t_zgl = P.dma("sync",
                          zg[:, h * TSL + bb * g.CW:h * TSL + (bb + 1) * g.CW],
                          cc_z_out.ap()[r * 512 + hlh * 128:r * 512 + (hlh + 1) * 128, :],
                          waits=[tCC2], inc="ZGL")
    tZGL = P.bar("ZGL")

    attn_fm = carve(96 * K, 32 * K, BF16)
    t_woev = {}
    for dc in range(NDC):
        for th in range(NTH):
            ii = dc * NTH + th
            bi = ii % 2
            tm = None
            for fc in range(NDC):
                tm = P.emit("tensor", lambda e, dc=dc, th=th, fc=fc, bi=bi: e.matmul(
                    pb[bi][:, :], woT[:, (fc * NDC + dc) * 128:(fc * NDC + dc + 1) * 128],
                    zg[:, fc * TSL + th * 512:fc * TSL + (th + 1) * 512],
                    start=(fc == 0), stop=(fc == NDC - 1)),
                    waits=[tZGL, tWOT] + pbg[bi].war(), inc="WOM")
            tev = P.emit("scalar", lambda e, dc=dc, th=th, bi=bi: e.copy(
                attn_fm[:, dc * TSL + th * 512:dc * TSL + (th + 1) * 512], pb[bi][:, :]),
                waits=[tm], inc="WOE")
            pbg[bi].used(tev)
            t_woev[ii] = tev
    tWOE = P.bar("WOE")
    if "attn_fm" in dbg_outs:
        P.dma("sync", dbg_outs["attn_fm"], attn_fm[:, 0:NDC * TSL],
              waits=[tWOE], inc="DBG")

    xfm2 = carve(0, 32 * K, BF16)
    xfm23 = xfm2.rearrange("p (hc t) -> p hc t", t=TSL)
    atile = [carve(32 * K, 4096, BF16), carve(36 * K, 4096, BF16)]
    xl2t = [carve(40 * K, 4096, BF16), carve(44 * K, 4096, BF16)]
    x1t = [carve(48 * K, 8192, F32), carve(56 * K, 8192, F32)]
    xtile = [carve(64 * K, 8192, F32), carve(72 * K, 8192, F32)]
    lntmp = carve(80 * K, 8192, F32)
    atg = Guard(2)
    x1g = Guard(2)
    xtg = Guard(2)
    lng = Guard(1)
    xf2g = Guard(1)

    t_x1s = {}
    for tt in range(NTT):
        ai = tt % 2
        tev = transpose16(
            lambda i, tt=tt: attn_fm[:, i * TSL + tt * 128:i * TSL + tt * 128 + 128],
            NDC, 6,
            lambda i0, cnt, ai=ai: atile[ai][:, i0 * 128:(i0 + cnt) * 128],
            waits=[tWOE], extra_first=atg.war(ai))
        tst = ln_stats(atile[ai][:, 0:D], D, 8 * (tt % 2), 1e-5, waits=[tev],
                       site=("lnp", tt % 2))
        tl = P.dma("sync", xtile[ai], x_sl[tt * 128:(tt + 1) * 128, :],
                   waits=[tWOE] + xtg.war(ai), inc=f"X2L{ai}")
        tnm = P.emit("vector", lambda e, ai=ai, c=8 * (tt % 2): e.tensor_scalar(
            lntmp[:, 0:D], atile[ai][:, 0:D], sstat[:, c:c + 1],
            sstat[:, c + 1:c + 2], ALU.mult, ALU.add),
            waits=[tst] + lng.war(), inc="LNP")
        atg.used(tnm, ai)
        ln_mark_done(("lnp", tt % 2), tnm)
        tx1 = P.emit("vector", lambda e, ai=ai: e.tensor_tensor(
            x1t[ai][:, 0:D], lntmp[:, 0:D], xtile[ai], ALU.add),
            waits=[tnm, tl] + x1g.war(ai), inc="X1")
        lng.used(tx1)
        xtg.used(tx1, ai)
        tst2 = ln_stats(x1t[ai][:, 0:D], D, 16 + 8 * (tt % 2), 1e-5, waits=[tx1],
                        site=("ln2", tt % 2))
        tn2 = P.emit("vector", lambda e, ai=ai, c=16 + 8 * (tt % 2): e.tensor_scalar(
            xl2t[ai][:, 0:D], x1t[ai][:, 0:D], sstat[:, c:c + 1],
            sstat[:, c + 1:c + 2], ALU.mult, ALU.add),
            waits=[tst2], inc="LN2N")
        ln_mark_done(("ln2", tt % 2), tn2)
        t_x1s[tt] = P.dma("sync", x1_spill[tt * 128:(tt + 1) * 128, :],
                          x1t[ai][:, 0:D], waits=[tn2], inc=f"X1S{ai}")
        x1g.used(t_x1s[tt], ai)
        txf = transpose16(
            lambda i, ai=ai: xl2t[ai][:, i * 128:(i + 1) * 128], NDC, 7,
            lambda i0, cnt, tt=tt: xfm23[:, i0:i0 + cnt, tt * 128:(tt + 1) * 128],
            waits=[tn2, tZGL, tWOE], extra_first=xf2g.war())
        xf2g.used(txf)
    tXF2 = P.bar("TREV")
    tX1S = P.bar("X1S0") + P.bar("X1S1")
    if "x1" in dbg_outs:
        P.dma("sync", dbg_outs["x1"], x1_spill, waits=[tX1S], inc="DBG")
    if "xfm2" in dbg_outs:
        P.dma("sync", dbg_outs["xfm2"], xfm2[:, 0:NDC * TSL], waits=[tXF2], inc="DBG")
    if upto <= 3:
        return _finish(P, nc, es, dbg_outs, None)

    h_fm = carve(32 * K, NJ * TSL * 2, BF16)
    gwb = [carve(120 * K, 4096, BF16), carve(124 * K, 4096, BF16)]
    uwb = [carve(128 * K, 4096, BF16), carve(132 * K, 4096, BF16)]
    stmp = [carve(136 * K, 2048, F32), carve(138 * K, 2048, F32)]
    gwg = Guard(2)
    uwg = Guard(2)
    stg = Guard(2)

    t_h = {}
    for j in range(NJ):
        bi = j % 2
        tg1 = P.dma("sync", gwb[bi][:, 0:D], cc_gu_out.ap()[j * 128:(j + 1) * 128, :],
                    waits=[tCC1, tX1S, tXF2] + gwg.war(bi), inc=f"GWLg{bi}")
        tg2 = P.dma("gpsimd", uwb[bi][:, 0:D],
                    cc_gu_out.ap()[(NJ + j) * 128:(NJ + j + 1) * 128, :],
                    waits=[tCC1, tX1S, tXF2] + uwg.war(bi), inc=f"GWLu{bi}")
        for th in range(NTH):
            tgm = tum = None
            for hc in range(NDC):
                tgm = P.emit("tensor", lambda e, j=j, th=th, hc=hc, bi=bi: e.matmul(
                    pb[bi][:, :], gwb[bi][:, hc * 128:(hc + 1) * 128],
                    xfm23[:, hc, th * 512:(th + 1) * 512],
                    start=(hc == 0), stop=(hc == NDC - 1)),
                    waits=[tg1, tXF2] + pbg[bi].war(), inc="GUM")
            for hc in range(NDC):
                tum = P.emit("tensor", lambda e, j=j, th=th, hc=hc, bi=bi: e.matmul(
                    pb[2 + bi][:, :], uwb[bi][:, hc * 128:(hc + 1) * 128],
                    xfm23[:, hc, th * 512:(th + 1) * 512],
                    start=(hc == 0), stop=(hc == NDC - 1)),
                    waits=[tg2, tXF2] + pbg[2 + bi].war(), inc="GUM")
            tsi = P.emit("scalar", lambda e, bi=bi, th=th: e.activation(
                stmp[th % 2][:, :], pb[bi][:, :], AF.Silu),
                waits=[tgm] + stg.war(th), inc="SIL")
            pbg[bi].used(tsi)
            th_ = P.emit("vector", lambda e, j=j, th=th, bi=bi: e.tensor_tensor(
                h_fm[:, j * TSL + th * 512:j * TSL + (th + 1) * 512],
                stmp[th % 2][:, :], pb[2 + bi][:, :], ALU.mult),
                waits=[tsi, tum], inc="HDN")
            pbg[2 + bi].used(th_)
            stg.used(th_, th)
            t_h[(j, th)] = th_
        gwg.used(t_h[(j, NTH - 1)], bi)
        uwg.used(t_h[(j, NTH - 1)], bi)
    tH = P.bar("HDN")
    if "h" in dbg_outs:
        P.dma("sync", dbg_outs["h"], h_fm[:, 0:NJ * TSL], waits=[tH], inc="DBG")

    wdn = [carve(120 * K, I * 2, BF16), carve(120 * K + ((I * 2 + 63) // 64) * 64, I * 2, BF16)]
    y_fm = carve(0, 32 * K, BF16)
    dwg = Guard(2)
    t_ye = {}
    for dc in range(NDC):
        bi = dc % 2
        tdw = P.dma("gpsimd", wdn[bi][:, 0:I], cc_dn_out.ap()[dc * 128:(dc + 1) * 128, :],
                    waits=[tCC1, tH] + dwg.war(bi), inc=f"DWL{bi}")
        tevp = None
        for th in range(NTH):
            pbi = 4 + (dc * NTH + th) % 2
            tm = None
            for j in range(NJ):
                tm = P.emit("tensor", lambda e, dc=dc, th=th, j=j, bi=bi, pbi=pbi: e.matmul(
                    pb[pbi][:, :], wdn[bi][:, j * 128:(j + 1) * 128],
                    h_fm[:, j * TSL + th * 512:j * TSL + (th + 1) * 512],
                    start=(j == 0), stop=(j == NJ - 1)),
                    waits=[tdw, tH] + pbg[pbi].war(), inc="DYM")
            tevp = P.emit("scalar", lambda e, dc=dc, th=th, pbi=pbi: e.copy(
                y_fm[:, dc * TSL + th * 512:dc * TSL + (th + 1) * 512], pb[pbi][:, :]),
                waits=[tm], inc="YEV")
            pbg[pbi].used(tevp)
        t_ye[dc] = tevp
        dwg.used(tevp, bi)
    tY = P.bar("YEV")
    if "y_fm" in dbg_outs:
        P.dma("sync", dbg_outs["y_fm"], y_fm[:, 0:NDC * TSL], waits=[tY], inc="DBG")

    lntmp2 = carve(32 * K, 8192, F32)
    x1l = [carve(40 * K, 8192, F32), carve(48 * K, 8192, F32)]
    ytile = [carve(56 * K, 4096, BF16), carve(60 * K, 4096, BF16)]
    ytg = Guard(2)
    x1lg = Guard(2)
    lng2 = Guard(1)
    t_out = {}
    for tt in range(NTT):
        ai = tt % 2
        tev = transpose16(
            lambda i, tt=tt: y_fm[:, i * TSL + tt * 128:i * TSL + tt * 128 + 128],
            NDC, 6, lambda i0, cnt, ai=ai: ytile[ai][:, i0 * 128:(i0 + cnt) * 128],
            waits=[tY], extra_first=ytg.war(ai))
        c = 8 * (tt % 2)
        war = st_war.get(("tail", tt % 2), [])
        s1, s2 = sstat[:, c + 2:c + 3], sstat[:, c + 3:c + 4]
        m_, ym2 = sstat[:, c + 4:c + 5], sstat[:, c + 5:c + 6]
        crms, rc = sstat[:, c + 6:c + 7], sstat[:, c + 7:c + 8]
        a2 = 32 + 4 * (tt % 2)
        var2, sig = sstat[:, a2:a2 + 1], sstat[:, a2 + 1:a2 + 2]
        alpha, beta = sstat[:, a2 + 2:a2 + 3], sstat[:, a2 + 3:a2 + 4]
        tv = P.emit("vector", lambda e, ai=ai: e.reduce_sum(
            s1, ytile[ai][:, 0:D], axis=AX.X), waits=[tev] + war, inc="LNS")
        ts = P.emit("scalar", lambda e, ai=ai: e.activation(
            sjunk[:, 0:D], ytile[ai][:, 0:D], AF.Square, accum_out=s2),
            waits=[tev] + war, inc="LNS")
        tv = P.emit("vector", lambda e: e.tensor_scalar(
            m_, s1, 1.0 / D, None, ALU.mult), waits=[tv], inc="LNS")
        tv = P.emit("vector", lambda e: e.tensor_scalar(
            ym2, s2, 1.0 / D, None, ALU.mult), waits=[tv, ts], inc="LNS")
        ts = P.emit("scalar", lambda e: e.activation(crms, ym2, AF.Sqrt, bias=1e-6),
                    waits=[tv], inc="LNS")
        tv = P.emit("vector", lambda e: e.reciprocal(rc, crms), waits=[ts], inc="LNS")
        tv = P.emit("vector", lambda e: e.tensor_tensor(var2, m_, m_, ALU.mult),
                    waits=[tv], inc="LNS")
        tv = P.emit("vector", lambda e: e.tensor_tensor(var2, ym2, var2, ALU.subtract),
                    waits=[tv], inc="LNS")
        tv = P.emit("vector", lambda e: e.tensor_tensor(sig, rc, rc, ALU.mult),
                    waits=[tv], inc="LNS")
        tv = P.emit("vector", lambda e: e.tensor_tensor(var2, var2, sig, ALU.mult),
                    waits=[tv], inc="LNS")
        ts = P.emit("scalar", lambda e: e.activation(sig, var2, AF.Sqrt, bias=1e-5),
                    waits=[tv], inc="LNS")
        tv = P.emit("vector", lambda e: e.reciprocal(var2, sig), waits=[ts], inc="LNS")
        tv = P.emit("vector", lambda e: e.tensor_tensor(alpha, rc, var2, ALU.mult),
                    waits=[tv], inc="LNS")
        tv = P.emit("vector", lambda e: e.tensor_scalar(
            beta, m_, var2, -1.0, ALU.mult, ALU.mult), waits=[tv], inc="LNS")
        tv = P.emit("vector", lambda e: e.tensor_tensor(beta, beta, rc, ALU.mult),
                    waits=[tv], inc="LNS")
        tl = P.dma("sync", x1l[ai], x1_spill[tt * 128:(tt + 1) * 128, :],
                   waits=[tX1S, tY] + x1lg.war(ai), inc=f"X1L{ai}")
        tapp = P.emit("vector", lambda e, ai=ai: e.tensor_scalar(
            lntmp2[:, 0:D], ytile[ai][:, 0:D], alpha, beta, ALU.mult, ALU.add),
            waits=[tv] + lng2.war(), inc="FAP")
        ytg.used(tapp, ai)
        st_war[("tail", tt % 2)] = [tapp]
        tfin = P.emit("vector", lambda e, ai=ai: e.tensor_tensor(
            lntmp2[:, 0:D], lntmp2[:, 0:D], x1l[ai], ALU.add),
            waits=[tapp, tl], inc="FIN")
        t_out[tt] = P.dma("sync", out_slab[tt * 128:(tt + 1) * 128, :], lntmp2[:, 0:D],
                          waits=[tfin], inc="OUT")
        lng2.used(t_out[tt])
        x1lg.used(t_out[tt], ai)
    return _finish(P, nc, es, dbg_outs, P.bar("OUT"))


def _finish(P, nc, es, dbg_outs, tOUT):
    if tOUT is not None:
        P.wait_only("sync", [tOUT])
    if dbg_outs:
        P.wait_only("sync", [P.bar("DBG")])
    block = es.enter_context(nc.Block())
    P.replay(block)
    es.close()
    return nc


def make_consts():
    consts = np.zeros((128, 16), dtype=np.float32)
    for v, i in CIDX.items():
        consts[:, i] = v
    ident = np.eye(128, dtype=np.float32).astype(BFNP)
    identf = np.eye(128, dtype=np.float32)
    p = np.arange(128)[:, None]
    m = np.arange(896)[None, :]
    tri = ((m - 384) >= p).astype(np.float32).astype(BFNP)
    onesrow = np.ones((1, 128), dtype=np.float32)
    return consts, ident, identf, tri, onesrow


def make_in_maps(g, x, W_Q, W_K, W_V, W_O, gup_full, gus_full, dnp_full, dns_full):
    consts, ident, identf, tri, onesrow = make_consts()
    expect = {
        "x_b": np.float32, "x_sl": np.float32, "wq": np.float32, "wk": np.float32,
        "wv": np.float32, "wo": np.float32, "gup": np.int32, "gus": np.float16,
        "dnp": np.int32, "dns": np.float16, "consts": np.float32,
        "ident": BFNP, "identf": np.float32, "tri": BFNP, "onesrow": np.float32,
    }
    in_maps = []
    for c in range(8):
        b, q = c // 4, c % 4
        m = {
            "x_b": np.ascontiguousarray(x[b]),
            "x_sl": np.ascontiguousarray(np.concatenate(
                [x[0][c * g.CW:(c + 1) * g.CW], x[1][c * g.CW:(c + 1) * g.CW]], 0)),
            "wq": np.ascontiguousarray(W_Q[4 * q:4 * q + 4].reshape(4 * g.DH, g.D)),
            "wk": np.ascontiguousarray(W_K[4 * q:4 * q + 4].reshape(4 * g.DH, g.D)),
            "wv": np.ascontiguousarray(W_V[4 * q:4 * q + 4].reshape(4 * g.DH, g.D)),
            "wo": np.ascontiguousarray(W_O),
            "gup": np.ascontiguousarray(gup_full[c * g.GU_PC:(c + 1) * g.GU_PC]),
            "gus": np.ascontiguousarray(gus_full[:, c * g.GU_PC:(c + 1) * g.GU_PC]),
            "dnp": np.ascontiguousarray(dnp_full[c * g.DN_PC:(c + 1) * g.DN_PC]),
            "dns": np.ascontiguousarray(dns_full[:, c * g.DN_PC:(c + 1) * g.DN_PC]),
            "consts": consts, "ident": ident, "identf": identf,
            "tri": tri, "onesrow": onesrow,
        }
        for k_, v_ in m.items():
            assert v_.dtype == expect[k_], (k_, v_.dtype)
        in_maps.append(m)
    return in_maps


def kernel(L0_attn_x, L0_attn_W_K, L0_attn_W_Q, L0_attn_W_V, L0_attn_W_O,
           L0_ffn_gate_up_packed, L0_ffn_gate_up_scales, L0_ffn_down_packed,
           L0_ffn_down_scales):
    g = FULL
    x = np.asarray(L0_attn_x)
    in_maps = make_in_maps(
        g, x, np.asarray(L0_attn_W_Q), np.asarray(L0_attn_W_K),
        np.asarray(L0_attn_W_V), np.asarray(L0_attn_W_O),
        np.asarray(L0_ffn_gate_up_packed), np.asarray(L0_ffn_gate_up_scales),
        np.asarray(L0_ffn_down_packed), np.asarray(L0_ffn_down_scales))
    nc = build_program(g)
    res = run_bass_kernel_spmd(nc, in_maps, core_ids=list(range(8)))
    out = np.empty((2, g.S, g.D), dtype=np.float32)
    for c in range(8):
        slab = res.results[c]["out_slab"]
        out[0, c * g.CW:(c + 1) * g.CW] = slab[0:g.CW]
        out[1, c * g.CW:(c + 1) * g.CW] = slab[g.CW:]
    return out
